# version 22
# baseline (speedup 1.0000x reference)
import os
import numpy as np
import ml_dtypes
from contextlib import ExitStack

import concourse.bass as bass
import concourse.mybir as mybir
from concourse.bass_utils import run_bass_kernel_spmd

F32 = mybir.dt.float32
BF16 = mybir.dt.bfloat16
AF = mybir.ActivationFunctionType
ALU = mybir.AluOpType
AX = mybir.AxisListType
BF = ml_dtypes.bfloat16

NCORE = 8


class Cfg:
    def __init__(self, S=8192, OWN=2048):
        self.S = S
        self.OWN = OWN
        self.D = 1024
        self.NCC = 8
        self.HP = 8
        self.FRAME = OWN + 1
        self.FPAD = ((self.FRAME + 127) // 128) * 128
        self.NSUB = self.FPAD // 128
        self.KT = S // 512
        self.KC = S // 128
        self.QT = []
        off = 0
        while off < self.OWN:
            n = min(512, self.OWN - off)
            self.QT.append((off, n))
            off += n
        self.QTA = self.QT + [(self.OWN, 128 if self.FPAD > self.OWN else 0)]
        self.QTA = [(o, n) for (o, n) in self.QTA if n > 0]
        self.DT = []
        off = 1
        while off < self.FRAME:
            n = min(512, self.FRAME - off)
            self.DT.append((off, n))
            off += n
        self.SELT = max(self.KT, len(self.QTA))


class Op:
    __slots__ = ("eng", "key", "val", "needed", "ph")

    def __init__(self, eng, key, ph):
        self.eng = eng
        self.key = key
        self.val = None
        self.needed = False
        self.ph = ph


class Trk:
    def __init__(self):
        self.ws = []
        self.rs = []

    def dep_read(self):
        return list(self.ws)

    def dep_write(self):
        return list(self.rs) if self.rs else list(self.ws)

    def note_read(self, ev):
        self.rs.append(ev)

    def note_write(self, ev):
        if self.rs:
            self.ws = [ev]
            self.rs = []
        else:
            self.ws.append(ev)


class Sched:
    ENGS = ("sync", "tensor", "vector", "scalar", "gpsimd")
    KEY = {"sync": "ld", "tensor": "pe", "vector": "dve",
           "scalar": "act", "gpsimd": "pool"}
    PHASE = [0]

    def __init__(self, counters, satisfied):
        self.q = {e: [] for e in self.ENGS}
        self.c = counters
        self.sat = satisfied
        Sched.PHASE[0] += 1
        self.ph = Sched.PHASE[0]

    def _mkop(self, eng, fn, deps, reads, writes, key, val, dma):
        op = Op(eng, key, self.ph)
        op.val = val
        for d in deps:
            if isinstance(d, Op):
                if d.ph == self.ph and d.eng != eng:
                    d.needed = True
        self.q[eng].append((deps, fn, op, dma))
        for t in reads:
            t.note_read(op)
        for t in writes:
            t.note_write(op)
        return op

    def dma_op(self, eng, fn, stream, reads=(), writes=(), parity=2, extra=()):
        idx = self.c.get(("dmaidx", stream), 0)
        self.c[("dmaidx", stream)] = idx + 1
        key = f"dm_{stream}{idx % parity}"
        prev = self.c.get(key, 0)
        deps = list(extra) + [("LIT", key, prev)]
        for t in reads:
            deps += t.dep_read()
        for t in writes:
            deps += t.dep_write()
        self.c[key] = prev + 16
        return self._mkop(eng, fn, deps, reads, writes, key, prev + 16, True)

    def op(self, eng, fn, reads=(), writes=(), extra=()):
        deps = list(extra)
        for t in reads:
            deps += t.dep_read()
        for t in writes:
            deps += t.dep_write()
        return self._mkop(eng, fn, deps, reads, writes,
                          self.KEY[eng], None, False)

    def raw(self, eng, fn, waits=(), inc=True):
        deps = [("LIT", k, v) for (k, v) in waits]
        self._mkop(eng, fn, deps, (), (), None, None, False)

    def emit(self, block, sems):
        for name in self.ENGS:
            key = self.KEY[name]
            for deps, fn, op, dma in self.q[name]:
                if not dma and op.key is not None and op.needed:
                    self.c[key] = self.c.get(key, 0) + 1
                    op.val = self.c[key]

        def body(name):
            def _(engine):
                sat = self.sat.setdefault(name, {})
                for deps, fn, op, dma in self.q[name]:
                    drain = False
                    waits = {}
                    for d in deps:
                        if isinstance(d, Op):
                            if d.ph != op.ph:
                                continue
                            if d.eng == name:
                                if name != "tensor":
                                    drain = True
                                continue
                            assert d.val is not None
                            if sat.get(d.key, 0) < d.val:
                                waits[d.key] = max(waits.get(d.key, 0), d.val)
                        else:
                            _, k, v = d
                            if v > 0 and sat.get(k, 0) < v:
                                waits[k] = max(waits.get(k, 0), v)
                    for k, v in sorted(waits.items()):
                        engine.wait_ge(sems[k], v)
                        sat[k] = v
                    if drain:
                        engine.drain()
                    r = fn(engine)
                    if op.val is not None and (dma or op.needed):
                        assert r is not None, f"missing instr on {name}"
                        r.then_inc(sems[op.key], 16 if dma else 1)
            return _
        for name in self.ENGS:
            if self.q[name]:
                getattr(block, name)(body(name))


def build_program(cfg):
    nc = bass.Bass("TRN2", target_bir_lowering=False)
    S, D, HP, NCC = cfg.S, cfg.D, cfg.HP, cfg.NCC
    FPAD, OWN, FRAME = cfg.FPAD, cfg.OWN, cfg.FRAME
    KT, KC, NSUB = cfg.KT, cfg.KC, cfg.NSUB
    NQT = len(cfg.QT)

    di = lambda n, sh, dt=F32: nc.dram_tensor(n, sh, dt, kind="ExternalInput").ap()
    x_own_d = di("x_own", [FPAD, D])
    x_full_d = di("x_full", [S, D])
    wq_d = di("wq", [D, D]); wk_d = di("wk", [D, D]); wv_d = di("wv", [D, D])
    wrt_d = di("wrt", [D, D]); wk2t_d = di("wk2t", [D, D]); wv2t_d = di("wv2t", [D, D])
    bq_d = di("bq_t", [128, HP]); bk_d = di("bk_t", [128, HP])
    bv_d = di("bv_t", [128, HP])
    mur_d = di("mur_t", [128, NCC]); muk_d = di("muk_t", [128, NCC])
    cos2_d = di("cos2", [128, S], BF16); sin2_d = di("sin2", [128, S], BF16)
    cosq_d = di("cosq", [128, FPAD], BF16); sinq_d = di("sinq", [128, FPAD], BF16)
    rt2_d = di("rt2", [128, 128], BF16)
    ident_d = di("ident", [128, 128], BF16)
    ident32_d = di("ident32", [128, 128])
    sel_d = di("sel", [128, cfg.SELT, 32], BF16)
    ones_d = di("ones_row", [1, 128], BF16)
    maskprev_d = di("maskprev", [128, 1])
    epsv_d = di("epsv", [128, 1])
    out_d = nc.dram_tensor("out", [OWN, D], F32, kind="ExternalOutput").ap()
    xT_sp = nc.dram_tensor("xT_spill", [128, KT * NCC * 512], BF16).ap()
    ctx_sp = nc.dram_tensor("ctx_spill", [NSUB, 128, HP * 128], F32).ap()

    C, SAT = {}, {}
    es = ExitStack()
    sems = {k: es.enter_context(nc.semaphore("sem_" + k))
            for k in ("ld", "g", "pe", "act", "dve", "pool")}

    class LazySems(dict):
        def __missing__(self, k):
            self[k] = es.enter_context(nc.semaphore("sem_" + k))
            return self[k]
    sems = LazySems(sems)

    def all_dma_waits():
        return [(k, v) for k, v in C.items()
                if isinstance(k, str) and k.startswith("dm_") and v > 0]

    def sbt(stack, n, sh, dt):
        return stack.enter_context(nc.sbuf_tensor(n, sh, dt))

    def pst(stack, n, sh, dt=F32):
        return stack.enter_context(nc.psum_tensor(n, sh, dt))

    bq_sb = sbt(es, "bq_sb", [128, HP], F32)
    bk_sb = sbt(es, "bk_sb", [128, HP], F32)
    bv_sb = sbt(es, "bv_sb", [128, HP], F32)
    mur_sb = sbt(es, "mur_sb", [128, NCC], F32)
    muk_sb = sbt(es, "muk_sb", [128, NCC], F32)
    omr_sb = sbt(es, "omr_sb", [128, NCC], F32)
    omk_sb = sbt(es, "omk_sb", [128, NCC], F32)
    rt2_sb = sbt(es, "rt2_sb", [128, 128], BF16)
    ident_sb = sbt(es, "ident_sb", [128, 128], BF16)
    ident32_sb = sbt(es, "ident32_sb", [128, 128], F32)
    sel_sb = sbt(es, "sel_sb", [128, cfg.SELT, 32], BF16)
    ones_sb = sbt(es, "ones_sb", [1, 128], BF16)
    maskprev_sb = sbt(es, "maskprev_sb", [128, 1], F32)
    epsv_sb = sbt(es, "epsv_sb", [128, 1], F32)
    qmax_sb = sbt(es, "qmax_sb", [1, 2 * HP], F32)
    mhat_sb = sbt(es, "mhat_sb", [128, HP], F32)
    mtmp_sb = sbt(es, "mtmp_sb", [1, 8], F32)
    mneg_bf = sbt(es, "mneg_bf", [1, 2], BF16)

    T = {}

    def trk(name):
        if name not in T:
            T[name] = Trk()
        return T[name]

    def dma(s, dst_ap, src_ap, stream, reads=(), writes=(), eng="sync",
            parity=2, extra=()):
        return s.dma_op(eng, lambda e: e.dma_start(out=dst_ap, in_=src_ap),
                        stream, reads=reads, writes=writes, parity=parity,
                        extra=extra)

    sA = Sched(C, SAT)
    esq = ExitStack()
    qT2 = sbt(esq, "qT2", [128, HP * FPAD], BF16)
    cos2_sb = sbt(esq, "cos2_sb", [128, S], BF16)
    sin2_sb = sbt(esq, "sin2_sb", [128, S], BF16)
    wkv_sb = [sbt(esq, "wkv_sb0", [128, 2 * NCC * 128], BF16), None]
    esa = ExitStack()
    wq_sb = sbt(esa, "wq_sb", [128, NCC * D], BF16)
    xoT = sbt(esa, "xoT", [128, NCC * FPAD], BF16)
    xst_f = [sbt(esa, f"xst_f{i}", [128, 4096], F32) for i in range(2)]
    xst_b = [sbt(esa, f"xst_b{i}", [128, 4096], BF16) for i in range(2)]
    qpre = [sbt(esa, f"qpre{i}", [128, 512], BF16) for i in range(4)]
    rq_sb = [sbt(esa, f"rq_sb{i}", [128, 512], BF16) for i in range(2)]
    t1_sb = sbt(esa, "t1_sb", [128, 512], BF16)
    t2_sb = sbt(esa, "t2_sb", [128, 512], BF16)
    qsq_sb = [sbt(esa, f"qsq_sb{i}", [128, 512], BF16) for i in range(4)]
    qn1_sb = sbt(esa, "qn1_sb", [32, 1], F32)
    cosq_sb = sbt(esa, "cosq_sb", [128, FPAD], BF16)
    sinq_sb = sbt(esa, "sinq_sb", [128, FPAD], BF16)
    xfT_st = [sbt(esa, f"xfT_st{i}", [128, NCC * 512], BF16) for i in range(2)]
    psa_tr = [pst(esa, f"psa_tr{i}", [128, 1024], BF16) for i in range(2)]
    psa_q = [pst(esa, f"psa_q{i}", [128, 512], F32) for i in range(3)]
    psa_u = pst(esa, "psa_u", [128, 512], F32)

    for dst, src in ((bq_sb, bq_d), (bk_sb, bk_d), (bv_sb, bv_d),
                     (mur_sb, mur_d), (muk_sb, muk_d), (rt2_sb, rt2_d),
                     (ident_sb, ident_d), (ident32_sb, ident32_d),
                     (sel_sb, sel_d), (ones_sb, ones_d),
                     (maskprev_sb, maskprev_d), (epsv_sb, epsv_d),
                     (cosq_sb, cosq_d),
                     (sinq_sb, sinq_d)):
        dma(sA, dst[tuple([slice(None)] * len(src.shape))], src, "cst",
            writes=[trk(f"c_{id(dst)}")], parity=4)
    for name, dst in (("bq", bq_sb), ("bk", bk_sb), ("bv", bv_sb),
                      ("mur", mur_sb), ("muk", muk_sb), ("rt2", rt2_sb),
                      ("ident", ident_sb), ("ident32", ident32_sb),
                      ("sel", sel_sb), ("ones", ones_sb),
                      ("maskprev", maskprev_sb), ("epsv", epsv_sb),
                      ("cosq", cosq_sb),
                      ("sinq", sinq_sb)):
        T[f"c_{name}"] = trk(f"c_{id(dst)}")
    sA.op("vector", lambda e: e.tensor_scalar(omr_sb[:, :], mur_sb[:, :],
                                              -1.0, 1.0, ALU.mult, ALU.add),
          reads=[T["c_mur"]], writes=[trk("omr")])
    sA.op("vector", lambda e: e.tensor_scalar(omk_sb[:, :], muk_sb[:, :],
                                              -1.0, 1.0, ALU.mult, ALU.add),
          reads=[T["c_muk"]], writes=[trk("omk")])

    for cc in range(NCC):
        b = cc % 2
        dma(sA, xst_f[b][:, 0:D], wq_d[cc * 128:(cc + 1) * 128, :], "wq",
            writes=[trk(f"xst_f{b}")])
        sA.op("vector", lambda e, b=b, cc=cc: e.tensor_copy(
            wq_sb[:, cc * D:(cc + 1) * D], xst_f[b][:, 0:D]),
            reads=[trk(f"xst_f{b}")], writes=[trk("wq_sb")])

    dma(sA, cos2_sb[:, :], cos2_d, "tbl", writes=[trk("cos2")])
    dma(sA, sin2_sb[:, :], sin2_d, "tbl", writes=[trk("sin2")])
    for j, wd in enumerate((wk_d, wv_d)):
        b = j % 2
        dma(sA, xst_f[b][:, 0:1024].rearrange("p (c m) -> p c m", m=128),
            wd[:, 0:128].rearrange("(c p) m -> p c m", p=128), "wq",
            writes=[trk(f"xst_f{b}")])
        sA.op("vector", lambda e, b=b, j=j: e.tensor_copy(
            wkv_sb[0][:, j * NCC * 128:(j + 1) * NCC * 128],
            xst_f[b][:, 0:1024]),
            reads=[trk(f"xst_f{b}")], writes=[trk("wkv0")])

    otiles = [(o, min(512, FPAD - o)) for o in range(0, FPAD, 512)]
    for i, (off, n) in enumerate(otiles):
        b = i % 2
        dma(sA, xst_f[b][:, 0:n * 8].rearrange("p (s d) -> p s d", d=1024),
            x_own_d[off:off + n, :].rearrange("(s p) d -> p s d", p=128), "xo",
            writes=[trk(f"xst_f{b}")])
        sA.op("vector", lambda e, b=b, n=n: e.tensor_copy(
            xst_b[b][:, 0:n * 8], xst_f[b][:, 0:n * 8]),
            reads=[trk(f"xst_f{b}")], writes=[trk(f"xst_b{b}")])
        for s in range(n // 128):
            pb = (i * 5 + s) % 2
            def trfn(e, b=b, s=s, pb=pb):
                r = None
                for cc in range(NCC):
                    r = e.matmul(psa_tr[pb][:, cc * 128:(cc + 1) * 128],
                                 xst_b[b][:, s * 1024 + cc * 128:
                                          s * 1024 + (cc + 1) * 128],
                                 ident_sb[:, :], start=True, stop=True,
                                 is_transpose=True)
                return r
            sA.op("tensor", trfn,
                  reads=[trk(f"xst_b{b}"), T["c_ident"]],
                  writes=[trk(f"psa_tr{pb}")])
            sA.op("vector", lambda e, pb=pb, off=off, s=s: e.tensor_copy(
                xoT.rearrange("p (c t) -> p c t", c=NCC)
                [:, :, off + s * 128:off + s * 128 + 128],
                psa_tr[pb][:, :].rearrange("p (c n) -> p c n", c=NCC)),
                reads=[trk(f"psa_tr{pb}")], writes=[trk("xoT")])

    xf_fillers = []
    for kt in range(KT):
        b = kt % 2
        def f_load(kt=kt, b=b):
            dma(sA, xst_f[b][:, :].rearrange("p (s d) -> p s d", d=1024),
                x_full_d[kt * 512:(kt + 1) * 512, :]
                .rearrange("(s p) d -> p s d", p=128), f"xf{kt % 2}",
                writes=[trk(f"xst_f{b}")])
        def f_cast(kt=kt, b=b):
            sA.op("scalar", lambda e, b=b: e.copy(
                xst_b[b][:, :], xst_f[b][:, :]),
                reads=[trk(f"xst_f{b}")], writes=[trk(f"xst_b{b}")])
        xf_fillers += [f_load, f_cast]
        for s in range(4):
            pb = (kt * 4 + s) % 2
            def f_tr(kt=kt, b=b, s=s, pb=pb):
                def trffn(e, b=b, s=s, pb=pb):
                    r = None
                    for cc in range(NCC):
                        r = e.matmul(psa_tr[pb][:, cc * 128:(cc + 1) * 128],
                                     xst_b[b][:, s * 1024 + cc * 128:
                                              s * 1024 + (cc + 1) * 128],
                                     ident_sb[:, :], start=True, stop=True,
                                     is_transpose=True)
                    return r
                sA.op("tensor", trffn, reads=[trk(f"xst_b{b}"), T["c_ident"]],
                      writes=[trk(f"psa_tr{pb}")])
                sA.op("scalar", lambda e, pb=pb, s=s, kt=kt: e.copy(
                    xfT_st[kt % 2][:, :].rearrange("p (c t) -> p c t", c=NCC)
                    [:, :, s * 128:(s + 1) * 128],
                    psa_tr[pb][:, :].rearrange("p (c n) -> p c n", c=NCC)),
                    reads=[trk(f"psa_tr{pb}")], writes=[trk(f"xfT_st{kt % 2}")])
            xf_fillers.append(f_tr)
        def f_sp(kt=kt):
            dma(sA, xT_sp[:, kt * NCC * 512:(kt + 1) * NCC * 512],
                xfT_st[kt % 2][:, :], "spw", reads=[trk(f"xfT_st{kt % 2}")],
                writes=[trk(f"xT_sp{kt}")], eng="gpsimd")
        xf_fillers.append(f_sp)
    NQTA = len(cfg.QTA)
    q_units = []
    for hp in range(HP):
        hp_units = []
        for qi, (qoff, qn) in enumerate(cfg.QTA):
            def q_front(hp=hp, qi=qi, qoff=qoff, qn=qn):
                pq = (hp * NQTA + qi) % 3
                pp = (hp * NQTA + qi) % 4
                def qproj(e, hp=hp, qoff=qoff, qn=qn, pq=pq):
                    r = None
                    for cc in range(NCC):
                        r = e.matmul(psa_q[pq][:, 0:qn],
                                     wq_sb[:, cc * D + hp * 128:
                                           cc * D + hp * 128 + 128],
                                     xoT.rearrange("p (c t) -> p c t", c=NCC)
                                     [:, cc, qoff:qoff + qn],
                                     start=(cc == 0), stop=(cc == NCC - 1))
                    return r
                sA.op("tensor", qproj, reads=[trk("wq_sb"), trk("xoT")],
                      writes=[trk(f"psa_q{pq}")])
                sA.op("scalar", lambda e, pq=pq, pp=pp, hp=hp, qn=qn:
                      e.activation(
                          qpre[pp][:, 0:qn], psa_q[pq][:, 0:qn], AF.Identity,
                          bias=bq_sb[:, hp:hp + 1], scale=1.0),
                      reads=[trk(f"psa_q{pq}"), T["c_bq"]],
                      writes=[trk(f"qpre{pp}")])

            def q_back(hp=hp, qi=qi, qoff=qoff, qn=qn):
                pq = (hp * NQTA + qi) % 3
                pp = (hp * NQTA + qi) % 4
                sA.op("tensor", lambda e, pq=pq, pp=pp, qn=qn: e.matmul(
                    psa_q[pq][:, 0:qn], rt2_sb[:, :], qpre[pp][:, 0:qn],
                    start=True, stop=True),
                    reads=[trk(f"qpre{pp}"), T["c_rt2"]],
                    writes=[trk(f"psa_q{pq}")])
                sA.op("vector", lambda e, pq=pq, pp=pp, qn=qn: e.tensor_copy(
                    rq_sb[pp % 2][:, 0:qn], psa_q[pq][:, 0:qn]),
                    reads=[trk(f"psa_q{pq}")], writes=[trk(f"rq{pp % 2}")])
                def rope_q(e, pp=pp, hp=hp, qoff=qoff, qn=qn):
                    e.tensor_tensor(t1_sb[:, 0:qn], rq_sb[pp % 2][:, 0:qn],
                                    sinq_sb[:, qoff:qoff + qn], ALU.mult)
                    e.tensor_tensor(t2_sb[:, 0:qn], qpre[pp][:, 0:qn],
                                    cosq_sb[:, qoff:qoff + qn], ALU.mult)
                    e.drain()
                    r = e.tensor_tensor(
                        qT2[:, hp * FPAD + qoff:hp * FPAD + qoff + qn],
                        t1_sb[:, 0:qn], t2_sb[:, 0:qn], ALU.add)
                    return r
                sA.op("vector", rope_q,
                      reads=[trk(f"rq{pp % 2}"), trk(f"qpre{pp}"), T["c_cosq"],
                             T["c_sinq"]],
                      writes=[trk("qT2"), trk("t12")])
                sA.op("gpsimd", lambda e, pp=pp, hp=hp, qoff=qoff, qn=qn:
                      e.tensor_tensor(
                          qsq_sb[pp][:, 0:qn],
                          qT2[:, hp * FPAD + qoff:hp * FPAD + qoff + qn],
                          qT2[:, hp * FPAD + qoff:hp * FPAD + qoff + qn],
                          ALU.mult),
                      reads=[trk("qT2")], writes=[trk(f"qsq{pp}")])
                sA.op("tensor", lambda e, qi=qi, qn=qn, pp=pp: e.matmul(
                    psa_u[0:32, 0:qn], sel_sb[:, qi, :], qsq_sb[pp][:, 0:qn],
                    start=(qi == 0), stop=(qi == NQTA - 1)),
                    reads=[trk(f"qsq{pp}"), T["c_sel"]], writes=[trk("psa_u")])
            hp_units.append((q_front, q_back))

        def q_tail(hp=hp):
            qtmax = max(n for _, n in cfg.QT)
            sA.op("vector", lambda e, qtmax=qtmax: e.reduce_max(
                qn1_sb[:, :], psa_u[0:32, 0:qtmax], axis=AX.X),
                reads=[trk("psa_u")], writes=[trk("qn1")])
            sA.op("tensor", lambda e: e.transpose(
                psa_q[0][0:1, 0:32], qn1_sb[:, :], ident32_sb[0:32, 0:32]),
                reads=[trk("qn1"), T["c_ident32"]], writes=[trk("psa_q0")])
            def qred(e, hp=hp):
                e.reduce_max(qmax_sb[:, 2 * hp:2 * hp + 1],
                             psa_q[0][0:1, 0:32]
                             .rearrange("p (n s) -> p n s", s=2)[:, :, 0],
                             axis=AX.X)
                e.drain()
                r = e.reduce_max(qmax_sb[:, 2 * hp + 1:2 * hp + 2],
                                 psa_q[0][0:1, 0:32]
                                 .rearrange("p (n s) -> p n s", s=2)[:, :, 1],
                                 axis=AX.X)
                return r
            sA.op("vector", qred, reads=[trk("psa_q0")], writes=[trk("qmax")])
        backlog = []
        for fr, bk in hp_units:
            q_units.append(fr)
            if len(backlog) >= 2:
                q_units.append(backlog.pop(0))
            backlog.append(bk)
        q_units.extend(backlog)
        q_units.append(q_tail)
        hp_units = []

    qu_iter = iter(q_units)
    rate = len(q_units) / max(1, len(xf_fillers))
    acc = 0.0
    for f in xf_fillers:
        f()
        acc += rate
        while acc >= 1.0:
            qf = next(qu_iter, None)
            if qf is not None:
                qf()
            acc -= 1.0
    for qf in qu_iter:
        qf()

    with nc.Block("phA") as blkA:
        sA.emit(blkA, sems)
    esa.close()

    sB = Sched(C, SAT)
    esb = ExitStack()
    kT2 = [sbt(esb, f"kT2_{i}", [128, S], BF16) for i in range(2)]
    wkv_sb[1] = sbt(esb, "wkv_sb1", [128, 2 * NCC * 128], BF16)
    v_aug = [sbt(esb, f"v_aug{i}", [128, KC * 130], BF16) for i in range(2)]
    xf_b = [sbt(esb, f"xf_b{i}", [128, NCC * 512], BF16) for i in range(2)]
    wkv_f = [sbt(esb, f"wkv_f{i}", [128, NCC * 128], F32) for i in range(2)]
    kpre_sb = [sbt(esb, f"kpre{i}", [128, 512], BF16) for i in range(2)]
    rk_sb = [sbt(esb, f"rk{i}", [128, 512], BF16) for i in range(2)]
    bt1 = sbt(esb, "bt1", [128, 512], BF16)
    bt2 = sbt(esb, "bt2", [128, 512], BF16)
    vpre_sb = [sbt(esb, f"vpre{i}", [128, 512], BF16) for i in range(2)]
    ksq_sb = [sbt(esb, f"ksq{i}", [128, 512], BF16) for i in range(4)]
    knp_sb = sbt(esb, "knp_sb", [32, 4], F32)
    kn1_sb = sbt(esb, "kn1_sb", [32, 1], F32)
    pt_sb = sbt(esb, "pt_sb", [128, 4 * 1024], BF16)
    ctxT_f = sbt(esb, "ctxT_f", [128, 512], F32)
    bcf_sb = sbt(esb, "bcf_sb", [128, 2], F32)
    bnd_sb = sbt(esb, "bnd_sb", [2, 64], F32)
    rl_sb = sbt(esb, "rl_sb", [128, 4], F32)
    ctx_hp = sbt(esb, "ctx_hp", [128, NSUB * 128], F32)
    ps_st = [pst(esb, "ps_stA", [128, 1024]), pst(esb, "ps_stB", [128, 1024])]
    ps_c = [pst(esb, f"ps_c{i}", [128, 512]) for i in range(2)]
    ps_b6 = pst(esb, "ps_b6", [128, 512])
    ps_b7 = pst(esb, "ps_b7", [128, 512], BF16)

    sB.raw("sync", lambda e: e.nop(), waits=all_dma_waits(), inc=False)

    def emit_load_wkv(hp):
        buf = hp % 2
        for j, wd in enumerate((wk_d, wv_d)):
            dma(sB, wkv_f[buf][:, :].rearrange("p (c m) -> p c m", m=128),
                wd[:, hp * 128:(hp + 1) * 128]
                .rearrange("(c p) m -> p c m", p=128), "wkv",
                writes=[trk(f"wkv_f{buf}")])
            sB.op("vector", lambda e, buf=buf, j=j: e.tensor_copy(
                wkv_sb[buf][:, j * NCC * 128:(j + 1) * NCC * 128],
                wkv_f[buf][:, :]),
                reads=[trk(f"wkv_f{buf}")], writes=[trk(f"wkv{buf}")])

    def kv_kt_fillers(hp, kt, tagged=False):
        buf = hp % 2
        xb = kt % 2
        pp = kt % 2
        fl = []
        fl.append(lambda: dma(sB, xf_b[xb][:, :],
            xT_sp[:, kt * NCC * 512:(kt + 1) * NCC * 512], "xft",
            reads=[trk(f"xT_sp{kt}")], writes=[trk(f"xf_b{xb}")]))
        def kproj(e, buf=buf, xb=xb):
            r = None
            for cc in range(NCC):
                r = e.matmul(ps_b6[:, :],
                             wkv_sb[buf][:, cc * 128:(cc + 1) * 128],
                             xf_b[xb][:, cc * 512:(cc + 1) * 512],
                             start=(cc == 0), stop=(cc == NCC - 1))
            return r
        def _fd(emit):
            def g():
                if os.environ.get("KDBG_FDRAIN"):
                    sB._fdrain = True
                emit()
                sB._fdrain = False
            return g
        fl.append(_fd(lambda: sB.op("tensor", kproj,
            reads=[trk(f"wkv{buf}"), trk(f"xf_b{xb}")], writes=[trk("b6")])))
        fl.append(lambda: sB.op("vector", lambda e, pp=pp, hp=hp: e.tensor_scalar(
            kpre_sb[pp][:, :], ps_b6[:, :], bk_sb[:, hp:hp + 1], None, ALU.add),
            reads=[trk("b6"), T["c_bk"]], writes=[trk(f"kpre{pp}")]))
        fl.append(_fd(lambda: sB.op("tensor", lambda e, pp=pp: e.matmul(
            ps_b6[:, :], rt2_sb[:, :], kpre_sb[pp][:, :], start=True, stop=True),
            reads=[trk(f"kpre{pp}"), T["c_rt2"]], writes=[trk("b6")])))
        fl.append(lambda: sB.op("vector", lambda e, pp=pp: e.tensor_copy(
            rk_sb[pp][:, :], ps_b6[:, :]),
            reads=[trk("b6")], writes=[trk(f"rk{pp}")]))
        def ropek(e, pp=pp, buf=buf, kt=kt):
            e.tensor_tensor(bt1[:, :], rk_sb[pp][:, :],
                            sin2_sb[:, kt * 512:(kt + 1) * 512], ALU.mult)
            e.tensor_tensor(bt2[:, :], kpre_sb[pp][:, :],
                            cos2_sb[:, kt * 512:(kt + 1) * 512], ALU.mult)
            e.drain()
            r = e.tensor_tensor(kT2[buf][:, kt * 512:(kt + 1) * 512],
                                bt1[:, :], bt2[:, :], ALU.add)
            return r
        fl.append(lambda: sB.op("vector", ropek,
            reads=[trk(f"rk{pp}"), trk(f"kpre{pp}"), trk("cos2"), trk("sin2")],
            writes=[trk(f"kT2_{buf}"), trk("btx")]))
        def vproj(e, buf=buf, xb=xb):
            r = None
            for cc in range(NCC):
                r = e.matmul(ps_b6[:, :],
                             wkv_sb[buf][:, NCC * 128 + cc * 128:
                                         NCC * 128 + (cc + 1) * 128],
                             xf_b[xb][:, cc * 512:(cc + 1) * 512],
                             start=(cc == 0), stop=(cc == NCC - 1))
            return r
        fl.append(_fd(lambda: sB.op("tensor", vproj,
            reads=[trk(f"wkv{buf}"), trk(f"xf_b{xb}")], writes=[trk("b6")])))
        fl.append(lambda: sB.op("vector", lambda e, pp=pp, hp=hp: e.tensor_scalar(
            vpre_sb[pp][:, :], ps_b6[:, :], bv_sb[:, hp:hp + 1], None, ALU.add),
            reads=[trk("b6"), T["c_bv"]], writes=[trk(f"vpre{pp}")]))
        def vtr(e, pp=pp):
            r = None
            for s in range(4):
                r = e.matmul(ps_b7[:, s * 128:(s + 1) * 128],
                             vpre_sb[pp][:, s * 128:(s + 1) * 128],
                             ident_sb[:, :], start=True, stop=True,
                             is_transpose=True)
            return r
        fl.append(_fd(lambda: sB.op("tensor", vtr,
            reads=[trk(f"vpre{pp}"), T["c_ident"]], writes=[trk("b7")])))
        def vsplit(e, buf=buf, kt=kt):
            r = None
            for s in range(4):
                c = kt * 4 + s
                r = e.tensor_copy(
                    v_aug[buf][:, c * 130:(c + 1) * 130]
                    .rearrange("p (g n) -> p g n", g=2)[:, :, 0:64],
                    ps_b7[:, s * 128:(s + 1) * 128]
                    .rearrange("p (g n) -> p g n", g=2))
            return r
        fl.append(lambda: sB.op("vector", vsplit, reads=[trk("b7")],
                                writes=[trk(f"v_aug{buf}")]))
        if tagged:
            tags = ["dma", "kproj", "kevac", "rkmm", "rkevac", "rope",
                    "vproj", "vevac", "vtr", "vsplit"]
            assert len(tags) == len(fl)
            return list(zip(fl, tags))
        return fl

    def kv_finale_fillers(hp, tagged=False):
        buf = hp % 2
        fl = []
        fl.append(lambda: sB.op("vector", lambda e, buf=buf: e.memset(
            v_aug[buf].rearrange("p (c x) -> p c x", c=KC)[:, :, 64:65], 1.0),
            writes=[trk(f"v_aug{buf}")]))
        fl.append(lambda: sB.op("vector", lambda e, buf=buf: e.memset(
            v_aug[buf].rearrange("p (c x) -> p c x", c=KC)[:, :, 129:130], 1.0),
            writes=[trk(f"v_aug{buf}")]))

        NG = 4
        for g in range(0, KT, NG):
            for kt in range(g, min(g + NG, KT)):
                pp = kt % NG
                fl.append(lambda kt=kt, pp=pp: sB.op("gpsimd",
                    lambda e, buf=buf, kt=kt, pp=pp: e.tensor_tensor(
                        ksq_sb[pp][:, :], kT2[buf][:, kt * 512:(kt + 1) * 512],
                        kT2[buf][:, kt * 512:(kt + 1) * 512], ALU.mult),
                    reads=[trk(f"kT2_{buf}")], writes=[trk(f"ksq{pp}")]))
            def grp(g=g):
                n = min(NG, KT - g)
                def mmg(e, g=g, n=n):
                    r = None
                    for kt in range(g, g + n):
                        r = e.matmul(ps_b6[0:32, 0:512], sel_sb[:, kt, :],
                                     ksq_sb[kt % NG][:, :],
                                     start=(kt == g), stop=(kt == g + n - 1))
                    return r
                sB.op("tensor", mmg,
                      reads=[trk(f"ksq{i}") for i in range(NG)] + [T["c_sel"]],
                      writes=[trk("b6")])
                sB.op("vector", lambda e, g=g: e.reduce_max(
                    knp_sb[:, g // NG:g // NG + 1], ps_b6[0:32, 0:512],
                    axis=AX.X),
                    reads=[trk("b6")], writes=[trk("knp")])
            fl.append(grp)
        def tail():
            sB.op("vector", lambda e: e.reduce_max(
                kn1_sb[:, :], knp_sb[:, 0:(KT + 3) // 4], axis=AX.X),
                reads=[trk("knp")], writes=[trk("kn1")])
            sB.op("tensor", lambda e: e.transpose(
                ps_b6[0:1, 0:32], kn1_sb[:, :], ident32_sb[0:32, 0:32]),
                reads=[trk("kn1"), T["c_ident32"]], writes=[trk("b6")])
            def mred(e, hp=hp):
                e.reduce_max(mtmp_sb[:, 0:1],
                             ps_b6[0:1, 0:32].rearrange("p (n s) -> p n s", s=2)
                             [:, :, 0], axis=AX.X)
                e.drain()
                e.reduce_max(mtmp_sb[:, 1:2],
                             ps_b6[0:1, 0:32].rearrange("p (n s) -> p n s", s=2)
                             [:, :, 1], axis=AX.X)
                e.drain()
                r = e.tensor_tensor(mtmp_sb[:, 2:4], qmax_sb[:, 2 * hp:2 * hp + 2],
                                    mtmp_sb[:, 0:2], ALU.mult)
                return r
            sB.op("vector", mred, reads=[trk("b6"), trk("qmax")],
                  writes=[trk("mtmp")])
            def mact(e):
                e.activation(mtmp_sb[:, 4:6], mtmp_sb[:, 2:4], AF.Ln,
                             bias=0.0, scale=1.0)
                e.drain()
                r = e.activation(mtmp_sb[:, 6:8], mtmp_sb[:, 4:6], AF.Exp,
                                 bias=0.0, scale=0.5)
                return r
            sB.op("scalar", mact, reads=[trk("mtmp")], writes=[trk("mtmp")])
            def mneg(e):
                e.reduce_max(mtmp_sb[:, 0:1], mtmp_sb[:, 6:8], axis=AX.X)
                e.drain()
                e.tensor_scalar(mtmp_sb[:, 1:2], mtmp_sb[:, 0:1],
                                -1.02 / 8.0, None, ALU.mult)
                e.drain()
                r = e.tensor_copy(mneg_bf[:, 0:1], mtmp_sb[:, 1:2])
                return r
            sB.op("vector", mneg, reads=[trk("mtmp")],
                  writes=[trk("mtmp"), trk("mnegb")])
            sB.op("tensor", lambda e: e.matmul(
                ps_b6[:, 0:1], ones_sb[:, :], mneg_bf[:, 0:1], start=True,
                stop=True),
                reads=[trk("mnegb"), T["c_ones"]], writes=[trk("b6")])
            sB.op("vector", lambda e, hp=hp: e.tensor_copy(
                mhat_sb[:, hp:hp + 1], ps_b6[:, 0:1]),
                reads=[trk("b6")], writes=[trk(f"mhat{hp}")])
        fl.append(tail)
        if tagged:
            tags_full = ["ms", "ms"]
            for g in range(0, KT, 4):
                tags_full += ["sq"] * min(4, KT - g) + ["grp"]
            tags_full += ["tail"]
            assert len(tags_full) == len(fl), (len(tags_full), len(fl))
            return list(zip(fl, tags_full))
        return fl

    def emit_attention_qtile(hp, qi, fillers=iter(()), per_c=0):
        qoff, qn = cfg.QT[qi]
        buf = hp % 2
        acc = 0.0
        for c in range(KC):
            acc += per_c
            while acc >= 1.0:
                f = next(fillers, None)
                if f is not None:
                    f()
                acc -= 1.0
            g = c % 2
            def qkpv(e, hp=hp, qi=qi, qoff=qoff, qn=qn, c=c, g=g, buf=buf):
                r = None
                for h in range(2):
                    for half in range(2):
                        r = e.matmul(
                            ps_st[g].rearrange("p (b n) -> p b n", b=2)
                            [half * 64:half * 64 + 64, h, 0:qn],
                            kT2[buf][h * 64:h * 64 + 64,
                                     c * 128 + half * 64:c * 128 + half * 64 + 64],
                            qT2[h * 64:h * 64 + 64,
                                hp * FPAD + qoff:hp * FPAD + qoff + qn],
                            start=True, stop=True)
                if c >= 2:
                    cp = c - 2
                    for h in range(2):
                        r = e.matmul(ps_c[h][0:65, 0:qn],
                                     v_aug[buf].rearrange("p (k x) -> p k x",
                                                          k=KC)
                                     [:, cp, h * 65:h * 65 + 65],
                                     pt_sb[:, (cp % 4) * 1024 + h * qn:
                                           (cp % 4) * 1024 + (h + 1) * qn],
                                     start=(cp == 0), stop=(cp == KC - 1))
                return r
            rd = [trk(f"kT2_{buf}"), trk("qT2")]
            wr = [trk(f"st{g}")]
            if c >= 2:
                rd += [trk(f"pt{(c - 2) % 4}"), trk(f"v_aug{buf}")]
                wr += [trk("ctx0"), trk("ctx1")]
            sB.op("tensor", qkpv, reads=rd, writes=wr)
            if qn == 512:
                sB.op("scalar", lambda e, g=g, c=c, hp=hp: e.activation(
                    pt_sb[:, (c % 4) * 1024:(c % 4) * 1024 + 1024],
                    ps_st[g][:, 0:1024],
                    AF.Exp, bias=mhat_sb[:, hp:hp + 1], scale=0.125),
                    reads=[trk(f"st{g}"), trk(f"mhat{hp}")],
                    writes=[trk(f"pt{c % 4}")])
            else:
                sB.op("scalar", lambda e, g=g, c=c, qn=qn, hp=hp: e.activation(
                    pt_sb[:, (c % 4) * 1024:(c % 4) * 1024 + 2 * qn]
                    .rearrange("p (b n) -> p b n", b=2),
                    ps_st[g].rearrange("p (b n) -> p b n", b=2)[:, :, 0:qn],
                    AF.Exp, bias=mhat_sb[:, hp:hp + 1], scale=0.125),
                    reads=[trk(f"st{g}"), trk(f"mhat{hp}")],
                    writes=[trk(f"pt{c % 4}")])
        _pv(hp, qi, KC - 2)
        _pv(hp, qi, KC - 1)
        nsl = (qn + 127) // 128
        for h in range(2):
            sB.op("vector", lambda e, h=h, qn=qn: e.tensor_copy(
                ctxT_f[0:65, 0:qn], ps_c[h][0:65, 0:qn]),
                reads=[trk(f"ctx{h}")], writes=[trk("ctxTf")])
            def ctr(e, h=h, nsl=nsl, qn=qn):
                r = None
                for s2 in range(nsl):
                    n2 = min(128, qn - s2 * 128)
                    r = e.matmul(ps_c[h][0:n2, s2 * 65:s2 * 65 + 65],
                                 ctxT_f[0:65, s2 * 128:s2 * 128 + n2],
                                 ident32_sb[0:65, 0:65], start=True, stop=True,
                                 is_transpose=True)
                return r
            sB.op("tensor", ctr, reads=[trk("ctxTf"), T["c_ident32"]],
                  writes=[trk(f"ctx{h}")])
            def cdiv(e, h=h, qi=qi, qn=qn, nsl=nsl):
                e.reciprocal(rl_sb[:, 0:nsl],
                             ps_c[h][:, 0:nsl * 65]
                             .rearrange("p (s x) -> p s x", x=65)[:, :, 64])
                e.drain()
                r = None
                for s2 in range(nsl):
                    r = e.tensor_scalar(
                        ctx_hp[:, (qi * 4 + s2) * 128 + h * 64:
                               (qi * 4 + s2) * 128 + h * 64 + 64],
                        ps_c[h][:, s2 * 65:s2 * 65 + 64],
                        rl_sb[:, s2:s2 + 1], None, ALU.mult)
                return r
            sB.op("vector", cdiv, reads=[trk(f"ctx{h}")],
                  writes=[trk("ctx_hp"), trk("rl")])

    def _pv(hp, qi, c):
        qoff, qn = cfg.QT[qi]
        buf = hp % 2
        def pv(e, c=c, qn=qn, buf=buf):
            r = None
            for h in range(2):
                r = e.matmul(ps_c[h][0:65, 0:qn],
                             v_aug[buf].rearrange("p (k x) -> p k x", k=KC)
                             [:, c, h * 65:h * 65 + 65],
                             pt_sb[:, (c % 4) * 1024 + h * qn:
                                   (c % 4) * 1024 + (h + 1) * qn],
                             start=(c == 0), stop=(c == KC - 1))
            return r
        wr = [trk("ctx0"), trk("ctx1")] if c in (0, KC - 1) else []
        ev = sB.op("tensor", pv,
                   reads=[trk(f"pt{c % 4}"), trk(f"v_aug{buf}")],
                   writes=wr)
        if not wr:
            trk("ctx0").note_write(ev)
            trk("ctx1").note_write(ev)

    def emit_boundary(hp):
        buf = hp % 2
        qc = hp * FPAD + OWN
        def bqk(e, buf=buf, qc=qc):
            r = None
            for c in range(KC):
                for h in range(2):
                    r = e.matmul(ps_st[0][:, h * 512 + c:h * 512 + c + 1],
                                 kT2[buf][h * 64:(h + 1) * 64,
                                          c * 128:(c + 1) * 128],
                                 qT2[h * 64:(h + 1) * 64, qc:qc + 1],
                                 start=True, stop=True)
            return r
        sB.op("tensor", bqk, reads=[trk(f"kT2_{buf}"), trk("qT2")],
              writes=[trk("st0")])
        sB.op("scalar", lambda e, hp=hp: e.activation(
            pt_sb[:, 0:2 * KC].rearrange("p (b n) -> p b n", b=2),
            ps_st[0].rearrange("p (b n) -> p b n", b=2)[:, :, 0:KC],
            AF.Exp, bias=mhat_sb[:, hp:hp + 1], scale=0.125),
            reads=[trk("st0"), trk(f"mhat{hp}")], writes=[trk("pt0")])
        for h in range(2):
            def bpv(e, h=h, buf=buf):
                r = None
                for c in range(KC):
                    r = e.matmul(ps_c[h][0:65, 0:1],
                                 v_aug[buf].rearrange("p (k x) -> p k x", k=KC)
                                 [:, c, h * 65:h * 65 + 65],
                                 pt_sb[:, h * KC + c:h * KC + c + 1],
                                 start=(c == 0), stop=(c == KC - 1))
                return r
            sB.op("tensor", bpv, reads=[trk("pt0"), trk(f"v_aug{buf}")],
                  writes=[trk(f"ctx{h}")])
        def bcopy(e):
            e.tensor_copy(bcf_sb[0:65, 0:1], ps_c[0][0:65, 0:1])
            e.drain()
            r = e.tensor_copy(bcf_sb[0:65, 1:2], ps_c[1][0:65, 0:1])
            return r
        sB.op("vector", bcopy, reads=[trk("ctx0"), trk("ctx1")],
              writes=[trk("bcf")])
        sB.op("tensor", lambda e: e.transpose(
            ps_b6[0:2, 0:65], bcf_sb[0:65, 0:2], ident32_sb[0:65, 0:65]),
            reads=[trk("bcf"), T["c_ident32"]], writes=[trk("b6")])
        def bdiv(e):
            e.reciprocal(rl_sb[0:2, 0:1], ps_b6[0:2, 64:65])
            e.drain()
            r = e.tensor_scalar(bnd_sb[0:2, 0:64], ps_b6[0:2, 0:64],
                                rl_sb[0:2, 0:1], None, ALU.mult)
            return r
        sB.op("vector", bdiv, reads=[trk("b6")],
              writes=[trk("bnd"), trk("rl")])
        sub16 = OWN // 128
        sB.op("vector", lambda e, sub16=sub16: e.memset(
            ctx_hp[:, sub16 * 128:(sub16 + 1) * 128], 0.0),
            writes=[trk("ctx_hp")])
        for h in range(2):
            dma(sB, ctx_hp[0:1, sub16 * 128 + h * 64:sub16 * 128 + h * 64 + 64],
                bnd_sb[h:h + 1, 0:64], "bnd",
                reads=[trk("bnd")], writes=[trk("ctx_hp")], eng="gpsimd")

    def emit_ctx_spill(hp):
        dma(sB, ctx_sp[:, :, hp * 128:(hp + 1) * 128]
            .rearrange("s p d -> p s d"),
            ctx_hp[:, :].rearrange("p (s d) -> p s d", d=128), "csp",
            reads=[trk("ctx_hp")], writes=[trk("ctx_sp")])

    def kv0_prologue():
        NG = 4
        for f, tag in kv_finale_fillers(0, tagged=True):
            if tag == "ms":
                f()
        for kt in range(KT):
            xb = kt % 2
            pp = kt % 2
            dma(sB, xf_b[xb][:, :],
                xT_sp[:, kt * NCC * 512:(kt + 1) * NCC * 512], "xft",
                reads=[trk(f"xT_sp{kt}")], writes=[trk(f"xf_b{xb}")])
            def kproj(e, xb=xb):
                r = None
                for cc in range(NCC):
                    r = e.matmul(ps_b6[:, :],
                                 wkv_sb[0][:, cc * 128:(cc + 1) * 128],
                                 xf_b[xb][:, cc * 512:(cc + 1) * 512],
                                 start=(cc == 0), stop=(cc == NCC - 1))
                return r
            sB.op("tensor", kproj, reads=[trk("wkv0"), trk(f"xf_b{xb}")],
                  writes=[trk("b6")])
            def vproj(e, xb=xb):
                r = None
                for cc in range(NCC):
                    r = e.matmul(ps_c[0][:, :],
                                 wkv_sb[0][:, NCC * 128 + cc * 128:
                                           NCC * 128 + (cc + 1) * 128],
                                 xf_b[xb][:, cc * 512:(cc + 1) * 512],
                                 start=(cc == 0), stop=(cc == NCC - 1))
                return r
            sB.op("tensor", vproj, reads=[trk("wkv0"), trk(f"xf_b{xb}")],
                  writes=[trk("ctx0")])
            sB.op("vector", lambda e, pp=pp: e.tensor_scalar(
                kpre_sb[pp][:, :], ps_b6[:, :], bk_sb[:, 0:1], None, ALU.add),
                reads=[trk("b6"), T["c_bk"]], writes=[trk(f"kpre{pp}")])
            sB.op("vector", lambda e, pp=pp: e.tensor_scalar(
                vpre_sb[pp][:, :], ps_c[0][:, :], bv_sb[:, 0:1], None,
                ALU.add),
                reads=[trk("ctx0"), T["c_bv"]], writes=[trk(f"vpre{pp}")])
            sB.op("tensor", lambda e, pp=pp: e.matmul(
                ps_b6[:, :], rt2_sb[:, :], kpre_sb[pp][:, :], start=True,
                stop=True),
                reads=[trk(f"kpre{pp}"), T["c_rt2"]], writes=[trk("b6")])
            def vtr(e, pp=pp):
                r = None
                for s in range(4):
                    r = e.matmul(ps_b7[:, s * 128:(s + 1) * 128],
                                 vpre_sb[pp][:, s * 128:(s + 1) * 128],
                                 ident_sb[:, :], start=True, stop=True,
                                 is_transpose=True)
                return r
            sB.op("tensor", vtr, reads=[trk(f"vpre{pp}"), T["c_ident"]],
                  writes=[trk("b7")])
            sB.op("vector", lambda e, pp=pp: e.tensor_copy(
                rk_sb[pp][:, :], ps_b6[:, :]),
                reads=[trk("b6")], writes=[trk(f"rk{pp}")])
            def ropek(e, pp=pp, kt=kt):
                e.tensor_tensor(bt1[:, :], rk_sb[pp][:, :],
                                sin2_sb[:, kt * 512:(kt + 1) * 512], ALU.mult)
                e.tensor_tensor(bt2[:, :], kpre_sb[pp][:, :],
                                cos2_sb[:, kt * 512:(kt + 1) * 512], ALU.mult)
                e.drain()
                r = e.tensor_tensor(kT2[0][:, kt * 512:(kt + 1) * 512],
                                    bt1[:, :], bt2[:, :], ALU.add)
                return r
            sB.op("vector", ropek,
                  reads=[trk(f"rk{pp}"), trk(f"kpre{pp}"), trk("cos2"),
                         trk("sin2")],
                  writes=[trk("kT2_0"), trk("btx")])
            sB.op("gpsimd", lambda e, kt=kt: e.tensor_tensor(
                ksq_sb[kt % NG][:, :], kT2[0][:, kt * 512:(kt + 1) * 512],
                kT2[0][:, kt * 512:(kt + 1) * 512], ALU.mult),
                reads=[trk("kT2_0")], writes=[trk(f"ksq{kt % NG}")])
            def vsplit(e, kt=kt):
                r = None
                for s in range(4):
                    c = kt * 4 + s
                    r = e.tensor_copy(
                        v_aug[0][:, c * 130:(c + 1) * 130]
                        .rearrange("p (g n) -> p g n", g=2)[:, :, 0:64],
                        ps_b7[:, s * 128:(s + 1) * 128]
                        .rearrange("p (g n) -> p g n", g=2))
                return r
            sB.op("vector", vsplit, reads=[trk("b7")], writes=[trk("v_aug0")])
            if kt % NG == NG - 1 or kt == KT - 1:
                g = kt - kt % NG
                n = kt % NG + 1
                def mmg(e, g=g, n=n):
                    r = None
                    for k2 in range(g, g + n):
                        r = e.matmul(ps_st[0][0:32, 0:512], sel_sb[:, k2, :],
                                     ksq_sb[k2 % NG][:, :],
                                     start=(k2 == g), stop=(k2 == g + n - 1))
                    return r
                sB.op("tensor", mmg,
                      reads=[trk(f"ksq{i}") for i in range(NG)] + [T["c_sel"]],
                      writes=[trk("st0")])
                sB.op("vector", lambda e, g=g: e.reduce_max(
                    knp_sb[:, g // NG:g // NG + 1], ps_st[0][0:32, 0:512],
                    axis=AX.X),
                    reads=[trk("st0")], writes=[trk("knp")])
        for f, tag in kv_finale_fillers(0, tagged=True):
            if tag == "tail":
                f()

    kv0_prologue()
    for hp in range(HP):
        if hp + 1 < HP:
            emit_load_wkv(hp + 1)
            postk = []
            fillers = []
            postset = set(os.environ.get('KDBG_POSTSET', '').split(','))
            if os.environ.get('KDBG_POSTK'):
                postset = {"dma", "kproj", "kevac", "rkmm", "rkevac", "rope",
                           "vproj", "vevac", "vtr", "vsplit"}
            for kt in range(KT):
                for f, tag in kv_kt_fillers(hp + 1, kt, tagged=True):
                    (postk if tag in postset else fillers).append(f)
            postf = []
            for f, tag in kv_finale_fillers(hp + 1, tagged=True):
                if tag in ("grp", "tail"):
                    postf.append(f)
                elif os.environ.get('KDBG_POSTK'):
                    postk.append(f)
                else:
                    fillers.append(f)
        else:
            fillers = []
            postf = []
            postk = []
        nslots = KC * NQT
        per_c = (len(fillers) + 2) / nslots
        if os.environ.get('KDBG_NOILB'):
            per_c = 0
        nb = os.environ.get('KDBG_NBOUND')
        if nb is not None:
            nb = int(nb)
            postk = fillers[nb:] + postk
            fillers = fillers[:nb]
        fit = iter(fillers)
        nper_qt = (len(fillers) + NQT - 1) // NQT
        for qi in range(NQT):
            if os.environ.get('KDBG_QTBOUND'):
                emit_attention_qtile(hp, qi, iter(()), 0)
                for _ in range(nper_qt):
                    f = next(fit, None)
                    if f is not None:
                        f()
            else:
                emit_attention_qtile(hp, qi, fit, per_c)
        for f in fit:
            f()
        emit_boundary(hp)
        for f in postk:
            f()
        for f in postf:
            f()
        emit_ctx_spill(hp)

    with nc.Block("phB") as blkB:
        sB.emit(blkB, sems)
    esb.close()
    esq.close()

    sC = Sched(C, SAT)
    esc = ExitStack()
    x1T = sbt(esc, "x1T", [128, NCC * FRAME], F32)
    wr_sb = sbt(esc, "wr_sb", [128, NCC * D], BF16)
    wk2_sb = sbt(esc, "wk2_sb", [128, NCC * D], BF16)
    wv2_sb = sbt(esc, "wv2_sb", [128, NCC * D], BF16)
    xo_c = [sbt(esc, f"xo_c{i}", [128, D], F32) for i in range(2)]
    ctxg = [sbt(esc, f"ctxg{i}", [128, HP * 128], F32) for i in range(2)]
    st6 = [sbt(esc, f"st6_{i}", [128, 12], F32) for i in range(2)]
    mv2 = [sbt(esc, f"mv2_{i}", [128, 2], F32) for i in range(2)]
    lnv = [sbt(esc, f"lnv_{i}", [128, 1], F32) for i in range(2)]
    rstd = [sbt(esc, f"rstd_{i}", [128, 1], F32) for i in range(2)]
    nmr = [sbt(esc, f"nmr_{i}", [128, 1], F32) for i in range(2)]
    xr_sb = [sbt(esc, f"xr_sb{i}", [128, NCC * 512], BF16) for i in range(2)]
    xk_sb = [sbt(esc, f"xk_sb{i}", [128, NCC * 512], BF16) for i in range(2)]
    tmp_f = sbt(esc, "tmp_f", [128, 512], F32)
    tmp_g = sbt(esc, "tmp_g", [128, 512], F32)
    r_sb = sbt(esc, "r_sb", [128, NCC * 512], BF16)
    krelu = sbt(esc, "krelu", [128, NCC * 512], BF16)
    fs_sb = [sbt(esc, f"fs_sb{i}", [128, 512], F32) for i in range(2)]
    kact = krelu
    sumc = [sbt(esc, f"sumc{i}", [128, 512], F32) for i in range(2)]
    outn = sbt(esc, "outn", [128, 4 * D], F32)
    outf = xo_c
    ps_d = [pst(esc, f"ps_d{i}", [128, 512]) for i in range(2)]
    ps_t = [pst(esc, f"ps_t{i}", [128, 512]) for i in range(2)]

    sC.raw("sync", lambda e: e.nop(), waits=all_dma_waits(), inc=False)
    sC.raw("vector", lambda e: e.nop(), waits=all_dma_waits(), inc=False)
    def load_wffn_all():
        for j2, (wd, dst, tn) in enumerate(((wrt_d, wr_sb, "wr"),
                                            (wk2t_d, wk2_sb, "wk2"),
                                            (wv2t_d, wv2_sb, "wv2"))):
            for cc in range(NCC):
                b = (j2 * NCC + cc) % 4
                dma(sC, outn[:, b * D:(b + 1) * D],
                    wd[cc * 128:(cc + 1) * 128, :], "wf",
                    writes=[trk("outn")], parity=4)
                sC.op("gpsimd", lambda e, b=b, dst=dst, cc=cc: e.tensor_copy(
                    dst[:, cc * D:(cc + 1) * D], outn[:, b * D:(b + 1) * D]),
                    reads=[trk("outn")], writes=[trk(tn)])

    load_wffn_all()

    def ln_stats(s, src_ap, extra_reads, par):
        def stats(e, par=par):
            e.bn_stats(st6[par][:, 0:6], src_ap[:, 0:512])
            e.bn_stats(st6[par][:, 6:12], src_ap[:, 512:1024])
            e.drain()
            r = e.bn_aggr(mv2[par][:, :], st6[par][:, :])
            return r
        s.op("vector", stats, reads=extra_reads, writes=[trk(f"lnstat{par}")])
        def rstdf(e, par=par):
            e.activation(lnv[par][:, :], mv2[par][:, 1:2], AF.Ln,
                         bias=epsv_sb[:, 0:1], scale=1.0)
            e.drain()
            r = e.activation(rstd[par][:, :], lnv[par][:, :], AF.Exp, bias=0.0,
                             scale=-0.5)
            return r
        s.op("scalar", rstdf, reads=[trk(f"lnstat{par}"), T["c_epsv"]],
             writes=[trk(f"lnstat{par}")])
        s.op("vector", lambda e, par=par: e.tensor_scalar(
            nmr[par][:, :], mv2[par][:, 0:1], rstd[par][:, 0:1], -1.0,
            ALU.mult, ALU.mult),
            reads=[trk(f"lnstat{par}")], writes=[trk(f"lnstat{par}")])

    def emit_csub(s):
        b = s % 2
        dma(sC, xo_c[b][:, :], x_own_d[s * 128:(s + 1) * 128, :], "xoc",
            writes=[trk(f"xo_c{b}")])
        dma(sC, ctxg[b][:, :], ctx_sp[s, :, :], "cg",
            reads=[trk("ctx_sp")], writes=[trk(f"ctxg{b}")])
        sC.op("vector", lambda e, b=b: e.tensor_tensor(
            ctxg[b][:, :], xo_c[b][:, :], ctxg[b][:, :], ALU.add),
            reads=[trk(f"xo_c{b}")], writes=[trk(f"ctxg{b}")])
        ln_stats(sC, ctxg[b], [trk(f"ctxg{b}")], b)
        sC.op("vector", lambda e, b=b: e.tensor_scalar(
            ctxg[b][:, :], ctxg[b][:, :], rstd[b][:, 0:1], nmr[b][:, 0:1],
            ALU.mult, ALU.add),
            reads=[trk(f"lnstat{b}")], writes=[trk(f"ctxg{b}")])
        for half in range(2):
            pb = half
            def trc(e, b=b, half=half, pb=pb):
                r = None
                for i4 in range(4):
                    cc = half * 4 + i4
                    r = e.matmul(ps_t[pb][:, i4 * 128:(i4 + 1) * 128],
                                 ctxg[b][:, cc * 128:(cc + 1) * 128],
                                 ident32_sb[:, :], start=True, stop=True,
                                 is_transpose=True)
                return r
            sC.op("tensor", trc, reads=[trk(f"ctxg{b}"), T["c_ident32"]],
                  writes=[trk(f"ps_t{pb}")])
            tc = min(128, FRAME - s * 128)
            sC.op("scalar", lambda e, s=s, half=half, pb=pb, tc=tc: e.copy(
                x1T.rearrange("p (c t) -> p c t", c=NCC)
                [:, half * 4:half * 4 + 4, s * 128:s * 128 + tc],
                ps_t[pb][:, :].rearrange("p (c n) -> p c n", c=4)[:, :, 0:tc]),
                reads=[trk(f"ps_t{pb}")], writes=[trk(f"x1T_{s}")])
        if s == 0:
            sC.op("vector", lambda e: e.tensor_scalar(
                x1T.rearrange("p (c t) -> p c t", c=NCC)[:, :, 0:1],
                x1T.rearrange("p (c t) -> p c t", c=NCC)[:, :, 0:1],
                maskprev_sb[:, 0:1], None, ALU.mult),
                reads=[T["c_maskprev"]], writes=[trk("x1T_0")])

    def emit_mix(j):
        doff, dn = cfg.DT[j]
        jb = j % 2
        lo = (doff - 1) // 128
        hi = (doff + dn - 1) // 128
        xtags = [trk(f"x1T_{s}") for s in range(lo, hi + 1)]
        for cc in range(NCC):
            db = cc % 2
            dbuf = (tmp_f, tmp_g)[db]
            cur = x1T.rearrange("p (c t) -> p c t", c=NCC)[:, cc,
                                                          doff:doff + dn]
            prv = x1T.rearrange("p (c t) -> p c t", c=NCC)[:, cc,
                                                          doff - 1:
                                                          doff - 1 + dn]
            sC.op("gpsimd", lambda e, dbuf=dbuf, dn=dn, cur=cur, prv=prv:
                  e.tensor_tensor(dbuf[:, 0:dn], cur, prv, ALU.subtract),
                  reads=xtags, writes=[trk(f"dmix{db}")])

            def stt2(e, dbuf=dbuf, dn=dn, cc=cc, jb=jb, prv=prv):
                e.scalar_tensor_tensor(
                    xr_sb[jb][:, cc * 512:cc * 512 + dn], dbuf[:, 0:dn],
                    mur_sb[:, cc:cc + 1], prv, ALU.mult, ALU.add)
                r = e.scalar_tensor_tensor(
                    xk_sb[jb][:, cc * 512:cc * 512 + dn], dbuf[:, 0:dn],
                    muk_sb[:, cc:cc + 1], prv, ALU.mult, ALU.add)
                return r
            sC.op("vector", stt2,
                  reads=[trk(f"dmix{db}"), T["c_mur"], T["c_muk"]] + xtags,
                  writes=[trk(f"xr{jb}"), trk(f"xk{jb}")])

    _sub_state = [0]

    def ensure_subs(k):
        while _sub_state[0] <= k and _sub_state[0] < NSUB:
            emit_csub(_sub_state[0])
            _sub_state[0] += 1

    def _hi(j):
        return (cfg.DT[j][0] + cfg.DT[j][1] - 1) // 128

    ensure_subs(_hi(0))
    emit_mix(0)
    for j, (doff, dn) in enumerate(cfg.DT):
        jp = j % 2
        if j + 1 < len(cfg.DT):
            ensure_subs(_hi(j + 1))
            emit_mix(j + 1)
        else:
            ensure_subs(NSUB - 1)
        for kind in range(2):
            mov, mtn = ((xr_sb[jp], f"xr{jp}"), (xk_sb[jp], f"xk{jp}"))[kind]
            fn_act = (AF.Sigmoid, AF.Relu)[kind]
            dst, dtn = ((r_sb, "r_sb"), (krelu, "krelu"))[kind]
            wmat, wtn = ((wr_sb, "wr"), (wk2_sb, "wk2"))[kind]
            for oc in range(NCC):
                pb = (kind * NCC + oc) % 2
                def ff(e, wmat=wmat, mov=mov, oc=oc, pb=pb, dn=dn):
                    r = None
                    for cc in range(NCC):
                        r = e.matmul(ps_d[pb][:, 0:dn],
                                     wmat[:, cc * D + oc * 128:
                                          cc * D + oc * 128 + 128],
                                     mov[:, cc * 512:cc * 512 + dn],
                                     start=(cc == 0), stop=(cc == NCC - 1))
                    return r
                sC.op("tensor", ff, reads=[trk(wtn), trk(mtn)],
                      writes=[trk(f"ps_d{pb}")])
                sC.op("scalar", lambda e, dst=dst, oc=oc, pb=pb, dn=dn,
                      fn_act=fn_act: e.activation(
                          dst[:, oc * 512:oc * 512 + dn], ps_d[pb][:, 0:dn],
                          fn_act, bias=0.0, scale=1.0),
                      reads=[trk(f"ps_d{pb}")], writes=[trk(dtn)])
                if kind == 1:
                    sC.op("gpsimd", lambda e, oc=oc, dn=dn:
                          e.tensor_tensor(
                              kact[:, oc * 512:oc * 512 + dn],
                              krelu[:, oc * 512:oc * 512 + dn],
                              krelu[:, oc * 512:oc * 512 + dn], ALU.mult),
                          reads=[trk("krelu")],
                          writes=[trk("kact")])
        for oc in range(NCC):
            pb = oc % 2
            def ov(e, oc=oc, pb=pb, dn=dn):
                r = None
                for dc in range(NCC):
                    r = e.matmul(ps_d[pb][:, 0:dn],
                                 wv2_sb[:, dc * D + oc * 128:
                                        dc * D + oc * 128 + 128],
                                 kact[:, dc * 512:dc * 512 + dn],
                                 start=(dc == 0), stop=(dc == NCC - 1))
                return r
            sC.op("tensor", ov, reads=[trk("wv2"), trk("kact")],
                  writes=[trk(f"ps_d{pb}")])
            sC.op("vector", lambda e, oc=oc, pb=pb, dn=dn: e.tensor_tensor(
                fs_sb[oc % 2][:, 0:dn], ps_d[pb][:, 0:dn],
                r_sb[:, oc * 512:oc * 512 + dn], ALU.mult),
                reads=[trk(f"ps_d{pb}"), trk("r_sb")],
                writes=[trk(f"fs{oc % 2}")])
            sC.op("gpsimd", lambda e, oc=oc, dn=dn, doff=doff: e.tensor_tensor(
                sumc[oc % 2][:, 0:dn], fs_sb[oc % 2][:, 0:dn],
                x1T.rearrange("p (c t) -> p c t", c=NCC)
                [:, oc, doff:doff + dn], ALU.add),
                reads=[trk(f"fs{oc % 2}")]
                + [trk(f"x1T_{s}") for s in range(doff // 128,
                                                  (doff + dn - 1) // 128 + 1)],
                writes=[trk(f"sumc{oc % 2}")])
            nss = dn // 128
            pb2 = oc % 2
            def trd(e, oc=oc, nss=nss, pb2=pb2):
                r = None
                for s2 in range(nss):
                    r = e.matmul(ps_t[pb2][:, s2 * 128:(s2 + 1) * 128],
                                 sumc[oc % 2][:, s2 * 128:(s2 + 1) * 128],
                                 ident32_sb[:, :], start=True, stop=True,
                                 is_transpose=True)
                return r
            sC.op("tensor", trd, reads=[trk(f"sumc{oc % 2}"), T["c_ident32"]],
                  writes=[trk(f"ps_t{pb2}")])
            sC.op("vector", lambda e, oc=oc, nss=nss, pb2=pb2: e.tensor_copy(
                outn.rearrange("p (s d) -> p s d", s=4)
                [:, 0:nss, oc * 128:(oc + 1) * 128],
                ps_t[pb2][:, 0:nss * 128].rearrange("p (s n) -> p s n", n=128)),
                reads=[trk(f"ps_t{pb2}")], writes=[trk("outn")])
        for s2 in range(dn // 128):
            b = s2 % 2
            def stats2(e, s2=s2, b=b):
                e.bn_stats(st6[b][:, 0:6], outn[:, s2 * D:s2 * D + 512])
                e.bn_stats(st6[b][:, 6:12], outn[:, s2 * D + 512:(s2 + 1) * D])
                e.drain()
                r = e.bn_aggr(mv2[b][:, :], st6[b][:, :])
                return r
            sC.op("vector", stats2, reads=[trk("outn")],
                  writes=[trk(f"lnstat{b}")])
            def rstdf2(e, b=b):
                e.activation(lnv[b][:, :], mv2[b][:, 1:2], AF.Ln,
                             bias=epsv_sb[:, 0:1], scale=1.0)
                e.drain()
                r = e.activation(rstd[b][:, :], lnv[b][:, :], AF.Exp, bias=0.0,
                                 scale=-0.5)
                return r
            sC.op("scalar", rstdf2, reads=[trk(f"lnstat{b}"), T["c_epsv"]],
                  writes=[trk(f"lnstat{b}")])
            def ofin(e, b=b, s2=s2):
                e.tensor_scalar(nmr[b][:, :], mv2[b][:, 0:1], rstd[b][:, 0:1],
                                -1.0, ALU.mult, ALU.mult)
                e.drain()
                r = e.tensor_scalar(outf[b][:, :], outn[:, s2 * D:(s2 + 1) * D],
                                    rstd[b][:, 0:1], nmr[b][:, 0:1], ALU.mult,
                                    ALU.add)
                return r
            sC.op("vector", ofin, reads=[trk(f"lnstat{b}"), trk("outn")],
                  writes=[trk(f"xo_c{b}"), trk(f"lnstat{b}")])
            dma(sC, out_d[j * 512 + s2 * 128:j * 512 + (s2 + 1) * 128, :],
                outf[b][:, :], "out", reads=[trk(f"xo_c{b}")],
                writes=[trk("outd")])

    sC.raw("sync", lambda e: e.nop(), waits=all_dma_waits(), inc=False)

    with nc.Block("phCD") as blkC:
        sC.emit(blkC, sems)
    esc.close()
    es.close()
    return nc


def _host_constants(cfg):
    S, HD = cfg.S, 64
    pos = np.arange(S, dtype=np.float32)
    inv = (10000.0 ** (-2.0 * np.arange(HD // 2, dtype=np.float32) / HD))
    ang = pos[:, None] * inv[None, :].astype(np.float32)
    cos = np.repeat(np.cos(ang), 2, axis=-1).astype(np.float32)
    sin = np.repeat(np.sin(ang), 2, axis=-1).astype(np.float32)
    cos2 = np.tile(cos.T, (2, 1)).astype(BF)
    sin2 = np.tile(sin.T, (2, 1)).astype(BF)
    R = np.zeros((64, 64), np.float32)
    for i in range(32):
        R[2 * i, 2 * i + 1] = -1.0
        R[2 * i + 1, 2 * i] = 1.0
    RT2 = np.zeros((128, 128), np.float32)
    RT2[0:64, 0:64] = R.T
    RT2[64:128, 64:128] = R.T
    ident = np.eye(128, dtype=np.float32)
    sel = np.zeros((128, cfg.SELT, 32), np.float32)
    for t in range(cfg.SELT):
        sel[0:64, t, 2 * t] = 1.0
        sel[64:128, t, 2 * t + 1] = 1.0
    ones_row = np.ones((1, 128), np.float32)
    return (cos, sin, cos2, sin2, RT2.astype(BF), ident.astype(BF), ident,
            sel.astype(BF), ones_row.astype(BF))


def make_in_maps(cfg, x, Wq, bq, Wk, bk, Wv, bv, W_r, W_k2, W_v2, mu_r, mu_k,
                 ncores=NCORE):
    cos, sin, cos2, sin2, RT2, ident, ident32, sel, ones_row = \
        _host_constants(cfg)
    HP = cfg.HP
    com = {
        "wq": np.ascontiguousarray(Wq), "wk": np.ascontiguousarray(Wk),
        "wv": np.ascontiguousarray(Wv),
        "wrt": np.ascontiguousarray(W_r.T),
        "wk2t": np.ascontiguousarray(W_k2.T),
        "wv2t": np.ascontiguousarray(W_v2.T),
        "bq_t": np.ascontiguousarray(bq.reshape(HP, 128).T),
        "bk_t": np.ascontiguousarray(bk.reshape(HP, 128).T),
        "bv_t": np.ascontiguousarray(bv.reshape(HP, 128).T),
        "mur_t": np.ascontiguousarray(mu_r.reshape(cfg.NCC, 128).T),
        "muk_t": np.ascontiguousarray(mu_k.reshape(cfg.NCC, 128).T),
        "cos2": cos2, "sin2": sin2, "rt2": RT2, "ident": ident,
        "ident32": ident32, "sel": sel, "ones_row": ones_row,
        "epsv": np.full((128, 1), 1e-5, np.float32),
    }
    in_maps = []
    nquart = ncores // x.shape[0]
    for core in range(ncores):
        b, qtr = core // nquart, core % nquart
        t0 = qtr * cfg.OWN
        xo = np.zeros((cfg.FPAD, cfg.D), np.float32)
        xo[1:cfg.OWN + 1] = x[b, t0:t0 + cfg.OWN]
        if qtr > 0:
            xo[0] = x[b, t0 - 1]
        posf = np.clip(np.arange(cfg.FPAD) + (t0 - 1), 0, cfg.S - 1)
        cosq = np.tile(cos[posf].T, (2, 1)).astype(BF)
        sinq = np.tile(sin[posf].T, (2, 1)).astype(BF)
        m = dict(com)
        m["x_own"] = xo
        m["x_full"] = np.ascontiguousarray(x[b])
        m["cosq"] = cosq
        m["sinq"] = sinq
        m["maskprev"] = np.full((128, 1), 0.0 if qtr == 0 else 1.0, np.float32)
        in_maps.append(m)
    return in_maps


def kernel(L0_attn_hidden_states, L0_attn_Wq, L0_attn_bq, L0_attn_Wk,
           L0_attn_bk, L0_attn_Wv, L0_attn_bv, L0_ffn_W_r, L0_ffn_W_k,
           L0_ffn_W_v, L0_ffn_mu_r, L0_ffn_mu_k):
    cfg = Cfg()
    x = np.ascontiguousarray(np.asarray(L0_attn_hidden_states, np.float32))
    assert x.shape == (2, cfg.S, cfg.D)
    args = [np.asarray(a, np.float32) for a in
            (L0_attn_Wq, L0_attn_bq, L0_attn_Wk, L0_attn_bk, L0_attn_Wv,
             L0_attn_bv, L0_ffn_W_r, L0_ffn_W_k, L0_ffn_W_v, L0_ffn_mu_r,
             L0_ffn_mu_k)]
    Wq, bq, Wk, bk, Wv, bv, W_r, W_k2, W_v2, mu_r, mu_k = args
    in_maps = make_in_maps(cfg, x, Wq, bq, Wk, bk, Wv, bv, W_r, W_k2, W_v2,
                           mu_r, mu_k)
    nc = build_program(cfg)
    res = run_bass_kernel_spmd(nc, in_maps, core_ids=list(range(NCORE)))
    out = np.empty((2, cfg.S, cfg.D), np.float32)
    for core in range(NCORE):
        b, qtr = core // 4, core % 4
        out[b, qtr * cfg.OWN:(qtr + 1) * cfg.OWN] = res.results[core]["out"]
    return out


# version 23
# speedup vs baseline: 1.0185x; 1.0185x over previous
import os
import numpy as np
import ml_dtypes
from contextlib import ExitStack

import concourse.bass as bass
import concourse.mybir as mybir
from concourse.bass_utils import run_bass_kernel_spmd

F32 = mybir.dt.float32
BF16 = mybir.dt.bfloat16
AF = mybir.ActivationFunctionType
ALU = mybir.AluOpType
AX = mybir.AxisListType
BF = ml_dtypes.bfloat16

NCORE = 8


class Cfg:
    def __init__(self, S=8192, OWN=2048):
        self.S = S
        self.OWN = OWN
        self.D = 1024
        self.NCC = 8
        self.HP = 8
        self.FRAME = OWN + 1
        self.FPAD = ((self.FRAME + 127) // 128) * 128
        self.NSUB = self.FPAD // 128
        self.KT = S // 512
        self.KC = S // 128
        self.QT = []
        off = 0
        while off < self.OWN:
            n = min(512, self.OWN - off)
            self.QT.append((off, n))
            off += n
        self.QTA = self.QT + [(self.OWN, 128 if self.FPAD > self.OWN else 0)]
        self.QTA = [(o, n) for (o, n) in self.QTA if n > 0]
        self.DT = []
        off = 1
        while off < self.FRAME:
            n = min(512, self.FRAME - off)
            self.DT.append((off, n))
            off += n
        self.SELT = max(self.KT, len(self.QTA))


class Op:
    __slots__ = ("eng", "key", "val", "needed", "ph")

    def __init__(self, eng, key, ph):
        self.eng = eng
        self.key = key
        self.val = None
        self.needed = False
        self.ph = ph


class Trk:
    def __init__(self):
        self.ws = []
        self.rs = []

    def dep_read(self):
        return list(self.ws)

    def dep_write(self):
        return list(self.rs) if self.rs else list(self.ws)

    def note_read(self, ev):
        self.rs.append(ev)

    def note_write(self, ev):
        if self.rs:
            self.ws = [ev]
            self.rs = []
        else:
            self.ws.append(ev)


class Sched:
    ENGS = ("sync", "tensor", "vector", "scalar", "gpsimd")
    KEY = {"sync": "ld", "tensor": "pe", "vector": "dve",
           "scalar": "act", "gpsimd": "pool"}
    PHASE = [0]

    def __init__(self, counters, satisfied):
        self.q = {e: [] for e in self.ENGS}
        self.c = counters
        self.sat = satisfied
        Sched.PHASE[0] += 1
        self.ph = Sched.PHASE[0]

    def _mkop(self, eng, fn, deps, reads, writes, key, val, dma):
        op = Op(eng, key, self.ph)
        op.val = val
        for d in deps:
            if isinstance(d, Op):
                if d.ph == self.ph and d.eng != eng:
                    d.needed = True
        self.q[eng].append((deps, fn, op, dma))
        for t in reads:
            t.note_read(op)
        for t in writes:
            t.note_write(op)
        return op

    def dma_op(self, eng, fn, stream, reads=(), writes=(), parity=2, extra=()):
        idx = self.c.get(("dmaidx", stream), 0)
        self.c[("dmaidx", stream)] = idx + 1
        key = f"dm_{stream}{idx % parity}"
        prev = self.c.get(key, 0)
        deps = list(extra) + [("LIT", key, prev)]
        for t in reads:
            deps += t.dep_read()
        for t in writes:
            deps += t.dep_write()
        self.c[key] = prev + 16
        return self._mkop(eng, fn, deps, reads, writes, key, prev + 16, True)

    def op(self, eng, fn, reads=(), writes=(), extra=()):
        deps = list(extra)
        for t in reads:
            deps += t.dep_read()
        for t in writes:
            deps += t.dep_write()
        return self._mkop(eng, fn, deps, reads, writes,
                          self.KEY[eng], None, False)

    def raw(self, eng, fn, waits=(), inc=True):
        deps = [("LIT", k, v) for (k, v) in waits]
        self._mkop(eng, fn, deps, (), (), None, None, False)

    def emit(self, block, sems):
        for name in self.ENGS:
            key = self.KEY[name]
            for deps, fn, op, dma in self.q[name]:
                if not dma and op.key is not None and op.needed:
                    self.c[key] = self.c.get(key, 0) + 1
                    op.val = self.c[key]

        def body(name):
            def _(engine):
                sat = self.sat.setdefault(name, {})
                for deps, fn, op, dma in self.q[name]:
                    drain = False
                    waits = {}
                    for d in deps:
                        if isinstance(d, Op):
                            if d.ph != op.ph:
                                continue
                            if d.eng == name:
                                if name != "tensor":
                                    drain = True
                                continue
                            assert d.val is not None
                            if sat.get(d.key, 0) < d.val:
                                waits[d.key] = max(waits.get(d.key, 0), d.val)
                        else:
                            _, k, v = d
                            if v > 0 and sat.get(k, 0) < v:
                                waits[k] = max(waits.get(k, 0), v)
                    for k, v in sorted(waits.items()):
                        engine.wait_ge(sems[k], v)
                        sat[k] = v
                    if drain:
                        engine.drain()
                    r = fn(engine)
                    if op.val is not None and (dma or op.needed):
                        assert r is not None, f"missing instr on {name}"
                        r.then_inc(sems[op.key], 16 if dma else 1)
            return _
        for name in self.ENGS:
            if self.q[name]:
                getattr(block, name)(body(name))


def build_program(cfg):
    nc = bass.Bass("TRN2", target_bir_lowering=False)
    S, D, HP, NCC = cfg.S, cfg.D, cfg.HP, cfg.NCC
    FPAD, OWN, FRAME = cfg.FPAD, cfg.OWN, cfg.FRAME
    KT, KC, NSUB = cfg.KT, cfg.KC, cfg.NSUB
    NQT = len(cfg.QT)

    di = lambda n, sh, dt=F32: nc.dram_tensor(n, sh, dt, kind="ExternalInput").ap()
    x_own_d = di("x_own", [FPAD, D])
    x_full_d = di("x_full", [S, D])
    wq_d = di("wq", [D, D]); wk_d = di("wk", [D, D]); wv_d = di("wv", [D, D])
    wrt_d = di("wrt", [D, D]); wk2t_d = di("wk2t", [D, D]); wv2t_d = di("wv2t", [D, D])
    bq_d = di("bq_t", [128, HP]); bk_d = di("bk_t", [128, HP])
    bv_d = di("bv_t", [128, HP])
    mur_d = di("mur_t", [128, NCC]); muk_d = di("muk_t", [128, NCC])
    cos2_d = di("cos2", [128, S], BF16); sin2_d = di("sin2", [128, S], BF16)
    cosq_d = di("cosq", [128, FPAD], BF16); sinq_d = di("sinq", [128, FPAD], BF16)
    rt2_d = di("rt2", [128, 128], BF16)
    ident_d = di("ident", [128, 128], BF16)
    ident32_d = di("ident32", [128, 128])
    sel_d = di("sel", [128, cfg.SELT, 32], BF16)
    ones_d = di("ones_row", [1, 128], BF16)
    maskprev_d = di("maskprev", [128, 1])
    epsv_d = di("epsv", [128, 1])
    out_d = nc.dram_tensor("out", [OWN, D], F32, kind="ExternalOutput").ap()
    xT_sp = nc.dram_tensor("xT_spill", [128, KT * NCC * 512], BF16).ap()
    ctx_sp = nc.dram_tensor("ctx_spill", [NSUB, 128, HP * 128], F32).ap()

    C, SAT = {}, {}
    es = ExitStack()
    sems = {k: es.enter_context(nc.semaphore("sem_" + k))
            for k in ("ld", "g", "pe", "act", "dve", "pool")}

    class LazySems(dict):
        def __missing__(self, k):
            self[k] = es.enter_context(nc.semaphore("sem_" + k))
            return self[k]
    sems = LazySems(sems)

    def all_dma_waits():
        return [(k, v) for k, v in C.items()
                if isinstance(k, str) and k.startswith("dm_") and v > 0]

    def sbt(stack, n, sh, dt):
        return stack.enter_context(nc.sbuf_tensor(n, sh, dt))

    def pst(stack, n, sh, dt=F32):
        return stack.enter_context(nc.psum_tensor(n, sh, dt))

    bq_sb = sbt(es, "bq_sb", [128, HP], F32)
    bk_sb = sbt(es, "bk_sb", [128, HP], F32)
    bv_sb = sbt(es, "bv_sb", [128, HP], F32)
    mur_sb = sbt(es, "mur_sb", [128, NCC], F32)
    muk_sb = sbt(es, "muk_sb", [128, NCC], F32)
    omr_sb = sbt(es, "omr_sb", [128, NCC], F32)
    omk_sb = sbt(es, "omk_sb", [128, NCC], F32)
    rt2_sb = sbt(es, "rt2_sb", [128, 128], BF16)
    ident_sb = sbt(es, "ident_sb", [128, 128], BF16)
    ident32_sb = sbt(es, "ident32_sb", [128, 128], F32)
    sel_sb = sbt(es, "sel_sb", [128, cfg.SELT, 32], BF16)
    ones_sb = sbt(es, "ones_sb", [1, 128], BF16)
    maskprev_sb = sbt(es, "maskprev_sb", [128, 1], F32)
    epsv_sb = sbt(es, "epsv_sb", [128, 1], F32)
    qmax_sb = sbt(es, "qmax_sb", [1, 2 * HP], F32)
    mhat_sb = sbt(es, "mhat_sb", [128, HP], F32)
    mtmp_sb = sbt(es, "mtmp_sb", [1, 8], F32)
    mneg_bf = sbt(es, "mneg_bf", [1, 2], BF16)

    T = {}

    def trk(name):
        if name not in T:
            T[name] = Trk()
        return T[name]

    def dma(s, dst_ap, src_ap, stream, reads=(), writes=(), eng="sync",
            parity=2, extra=()):
        return s.dma_op(eng, lambda e: e.dma_start(out=dst_ap, in_=src_ap),
                        stream, reads=reads, writes=writes, parity=parity,
                        extra=extra)

    sA = Sched(C, SAT)
    esq = ExitStack()
    qT2 = sbt(esq, "qT2", [128, HP * FPAD], BF16)
    cos2_sb = sbt(esq, "cos2_sb", [128, S], BF16)
    sin2_sb = sbt(esq, "sin2_sb", [128, S], BF16)
    wkv_sb = [sbt(esq, "wkv_sb0", [128, 2 * NCC * 128], BF16), None]
    esa = ExitStack()
    wq_sb = sbt(esa, "wq_sb", [128, NCC * D], BF16)
    xoT = sbt(esa, "xoT", [128, NCC * FPAD], BF16)
    xst_f = [sbt(esa, f"xst_f{i}", [128, 4096], F32) for i in range(2)]
    xst_b = [sbt(esa, f"xst_b{i}", [128, 4096], BF16) for i in range(2)]
    qpre = [sbt(esa, f"qpre{i}", [128, 512], BF16) for i in range(4)]
    rq_sb = [sbt(esa, f"rq_sb{i}", [128, 512], BF16) for i in range(2)]
    t1_sb = sbt(esa, "t1_sb", [128, 512], BF16)
    t2_sb = sbt(esa, "t2_sb", [128, 512], BF16)
    qsq_sb = [sbt(esa, f"qsq_sb{i}", [128, 512], BF16) for i in range(4)]
    qn1_sb = sbt(esa, "qn1_sb", [32, 1], F32)
    cosq_sb = sbt(esa, "cosq_sb", [128, FPAD], BF16)
    sinq_sb = sbt(esa, "sinq_sb", [128, FPAD], BF16)
    xfT_st = [sbt(esa, f"xfT_st{i}", [128, NCC * 512], BF16) for i in range(2)]
    psa_tr = [pst(esa, f"psa_tr{i}", [128, 1024], BF16) for i in range(2)]
    psa_q = [pst(esa, f"psa_q{i}", [128, 512], F32) for i in range(3)]
    psa_u = pst(esa, "psa_u", [128, 512], F32)

    for dst, src in ((bq_sb, bq_d), (bk_sb, bk_d), (bv_sb, bv_d),
                     (mur_sb, mur_d), (muk_sb, muk_d), (rt2_sb, rt2_d),
                     (ident_sb, ident_d), (ident32_sb, ident32_d),
                     (sel_sb, sel_d), (ones_sb, ones_d),
                     (maskprev_sb, maskprev_d), (epsv_sb, epsv_d),
                     (cosq_sb, cosq_d),
                     (sinq_sb, sinq_d)):
        dma(sA, dst[tuple([slice(None)] * len(src.shape))], src, "cst",
            writes=[trk(f"c_{id(dst)}")], parity=4)
    for name, dst in (("bq", bq_sb), ("bk", bk_sb), ("bv", bv_sb),
                      ("mur", mur_sb), ("muk", muk_sb), ("rt2", rt2_sb),
                      ("ident", ident_sb), ("ident32", ident32_sb),
                      ("sel", sel_sb), ("ones", ones_sb),
                      ("maskprev", maskprev_sb), ("epsv", epsv_sb),
                      ("cosq", cosq_sb),
                      ("sinq", sinq_sb)):
        T[f"c_{name}"] = trk(f"c_{id(dst)}")
    sA.op("vector", lambda e: e.tensor_scalar(omr_sb[:, :], mur_sb[:, :],
                                              -1.0, 1.0, ALU.mult, ALU.add),
          reads=[T["c_mur"]], writes=[trk("omr")])
    sA.op("vector", lambda e: e.tensor_scalar(omk_sb[:, :], muk_sb[:, :],
                                              -1.0, 1.0, ALU.mult, ALU.add),
          reads=[T["c_muk"]], writes=[trk("omk")])

    for cc in range(NCC):
        b = cc % 2
        dma(sA, xst_f[b][:, 0:D], wq_d[cc * 128:(cc + 1) * 128, :], "wq",
            writes=[trk(f"xst_f{b}")])
        sA.op("vector", lambda e, b=b, cc=cc: e.tensor_copy(
            wq_sb[:, cc * D:(cc + 1) * D], xst_f[b][:, 0:D]),
            reads=[trk(f"xst_f{b}")], writes=[trk("wq_sb")])

    dma(sA, cos2_sb[:, :], cos2_d, "tbl", writes=[trk("cos2")])
    dma(sA, sin2_sb[:, :], sin2_d, "tbl", writes=[trk("sin2")])
    for j, wd in enumerate((wk_d, wv_d)):
        b = j % 2
        dma(sA, xst_f[b][:, 0:1024].rearrange("p (c m) -> p c m", m=128),
            wd[:, 0:128].rearrange("(c p) m -> p c m", p=128), "wq",
            writes=[trk(f"xst_f{b}")])
        sA.op("vector", lambda e, b=b, j=j: e.tensor_copy(
            wkv_sb[0][:, j * NCC * 128:(j + 1) * NCC * 128],
            xst_f[b][:, 0:1024]),
            reads=[trk(f"xst_f{b}")], writes=[trk("wkv0")])

    otiles = [(o, min(512, FPAD - o)) for o in range(0, FPAD, 512)]
    for i, (off, n) in enumerate(otiles):
        b = i % 2
        dma(sA, xst_f[b][:, 0:n * 8].rearrange("p (s d) -> p s d", d=1024),
            x_own_d[off:off + n, :].rearrange("(s p) d -> p s d", p=128), "xo",
            writes=[trk(f"xst_f{b}")])
        sA.op("vector", lambda e, b=b, n=n: e.tensor_copy(
            xst_b[b][:, 0:n * 8], xst_f[b][:, 0:n * 8]),
            reads=[trk(f"xst_f{b}")], writes=[trk(f"xst_b{b}")])
        for s in range(n // 128):
            pb = (i * 5 + s) % 2
            def trfn(e, b=b, s=s, pb=pb):
                r = None
                for cc in range(NCC):
                    r = e.matmul(psa_tr[pb][:, cc * 128:(cc + 1) * 128],
                                 xst_b[b][:, s * 1024 + cc * 128:
                                          s * 1024 + (cc + 1) * 128],
                                 ident_sb[:, :], start=True, stop=True,
                                 is_transpose=True)
                return r
            sA.op("tensor", trfn,
                  reads=[trk(f"xst_b{b}"), T["c_ident"]],
                  writes=[trk(f"psa_tr{pb}")])
            sA.op("vector", lambda e, pb=pb, off=off, s=s: e.tensor_copy(
                xoT.rearrange("p (c t) -> p c t", c=NCC)
                [:, :, off + s * 128:off + s * 128 + 128],
                psa_tr[pb][:, :].rearrange("p (c n) -> p c n", c=NCC)),
                reads=[trk(f"psa_tr{pb}")], writes=[trk("xoT")])

    xf_fillers = []
    for kt in range(KT):
        b = kt % 2
        def f_load(kt=kt, b=b):
            dma(sA, xst_f[b][:, :].rearrange("p (s d) -> p s d", d=1024),
                x_full_d[kt * 512:(kt + 1) * 512, :]
                .rearrange("(s p) d -> p s d", p=128), f"xf{kt % 2}",
                writes=[trk(f"xst_f{b}")])
        def f_cast(kt=kt, b=b):
            sA.op("scalar", lambda e, b=b: e.copy(
                xst_b[b][:, :], xst_f[b][:, :]),
                reads=[trk(f"xst_f{b}")], writes=[trk(f"xst_b{b}")])
        xf_fillers += [f_load, f_cast]
        for s in range(4):
            pb = (kt * 4 + s) % 2
            def f_tr(kt=kt, b=b, s=s, pb=pb):
                def trffn(e, b=b, s=s, pb=pb):
                    r = None
                    for cc in range(NCC):
                        r = e.matmul(psa_tr[pb][:, cc * 128:(cc + 1) * 128],
                                     xst_b[b][:, s * 1024 + cc * 128:
                                              s * 1024 + (cc + 1) * 128],
                                     ident_sb[:, :], start=True, stop=True,
                                     is_transpose=True)
                    return r
                sA.op("tensor", trffn, reads=[trk(f"xst_b{b}"), T["c_ident"]],
                      writes=[trk(f"psa_tr{pb}")])
                sA.op("scalar", lambda e, pb=pb, s=s, kt=kt: e.copy(
                    xfT_st[kt % 2][:, :].rearrange("p (c t) -> p c t", c=NCC)
                    [:, :, s * 128:(s + 1) * 128],
                    psa_tr[pb][:, :].rearrange("p (c n) -> p c n", c=NCC)),
                    reads=[trk(f"psa_tr{pb}")], writes=[trk(f"xfT_st{kt % 2}")])
            xf_fillers.append(f_tr)
        def f_sp(kt=kt):
            dma(sA, xT_sp[:, kt * NCC * 512:(kt + 1) * NCC * 512],
                xfT_st[kt % 2][:, :], "spw", reads=[trk(f"xfT_st{kt % 2}")],
                writes=[trk(f"xT_sp{kt}")], eng="gpsimd")
        xf_fillers.append(f_sp)
    NQTA = len(cfg.QTA)
    q_units = []
    for hp in range(HP):
        hp_units = []
        for qi, (qoff, qn) in enumerate(cfg.QTA):
            def q_front(hp=hp, qi=qi, qoff=qoff, qn=qn):
                pq = (hp * NQTA + qi) % 3
                pp = (hp * NQTA + qi) % 4
                def qproj(e, hp=hp, qoff=qoff, qn=qn, pq=pq):
                    r = None
                    for cc in range(NCC):
                        r = e.matmul(psa_q[pq][:, 0:qn],
                                     wq_sb[:, cc * D + hp * 128:
                                           cc * D + hp * 128 + 128],
                                     xoT.rearrange("p (c t) -> p c t", c=NCC)
                                     [:, cc, qoff:qoff + qn],
                                     start=(cc == 0), stop=(cc == NCC - 1))
                    return r
                sA.op("tensor", qproj, reads=[trk("wq_sb"), trk("xoT")],
                      writes=[trk(f"psa_q{pq}")])
                sA.op("scalar", lambda e, pq=pq, pp=pp, hp=hp, qn=qn:
                      e.activation(
                          qpre[pp][:, 0:qn], psa_q[pq][:, 0:qn], AF.Identity,
                          bias=bq_sb[:, hp:hp + 1], scale=1.0),
                      reads=[trk(f"psa_q{pq}"), T["c_bq"]],
                      writes=[trk(f"qpre{pp}")])

            def q_back(hp=hp, qi=qi, qoff=qoff, qn=qn):
                pq = (hp * NQTA + qi) % 3
                pp = (hp * NQTA + qi) % 4
                sA.op("tensor", lambda e, pq=pq, pp=pp, qn=qn: e.matmul(
                    psa_q[pq][:, 0:qn], rt2_sb[:, :], qpre[pp][:, 0:qn],
                    start=True, stop=True),
                    reads=[trk(f"qpre{pp}"), T["c_rt2"]],
                    writes=[trk(f"psa_q{pq}")])
                sA.op("vector", lambda e, pq=pq, pp=pp, qn=qn: e.tensor_copy(
                    rq_sb[pp % 2][:, 0:qn], psa_q[pq][:, 0:qn]),
                    reads=[trk(f"psa_q{pq}")], writes=[trk(f"rq{pp % 2}")])
                def rope_q(e, pp=pp, hp=hp, qoff=qoff, qn=qn):
                    e.tensor_tensor(t1_sb[:, 0:qn], rq_sb[pp % 2][:, 0:qn],
                                    sinq_sb[:, qoff:qoff + qn], ALU.mult)
                    e.tensor_tensor(t2_sb[:, 0:qn], qpre[pp][:, 0:qn],
                                    cosq_sb[:, qoff:qoff + qn], ALU.mult)
                    e.drain()
                    r = e.tensor_tensor(
                        qT2[:, hp * FPAD + qoff:hp * FPAD + qoff + qn],
                        t1_sb[:, 0:qn], t2_sb[:, 0:qn], ALU.add)
                    return r
                sA.op("vector", rope_q,
                      reads=[trk(f"rq{pp % 2}"), trk(f"qpre{pp}"), T["c_cosq"],
                             T["c_sinq"]],
                      writes=[trk("qT2"), trk("t12")])
                sA.op("gpsimd", lambda e, pp=pp, hp=hp, qoff=qoff, qn=qn:
                      e.tensor_tensor(
                          qsq_sb[pp][:, 0:qn],
                          qT2[:, hp * FPAD + qoff:hp * FPAD + qoff + qn],
                          qT2[:, hp * FPAD + qoff:hp * FPAD + qoff + qn],
                          ALU.mult),
                      reads=[trk("qT2")], writes=[trk(f"qsq{pp}")])
                sA.op("tensor", lambda e, qi=qi, qn=qn, pp=pp: e.matmul(
                    psa_u[0:32, 0:qn], sel_sb[:, qi, :], qsq_sb[pp][:, 0:qn],
                    start=(qi == 0), stop=(qi == NQTA - 1)),
                    reads=[trk(f"qsq{pp}"), T["c_sel"]], writes=[trk("psa_u")])
            hp_units.append((q_front, q_back))

        def q_tail(hp=hp):
            qtmax = max(n for _, n in cfg.QT)
            sA.op("vector", lambda e, qtmax=qtmax: e.reduce_max(
                qn1_sb[:, :], psa_u[0:32, 0:qtmax], axis=AX.X),
                reads=[trk("psa_u")], writes=[trk("qn1")])
            sA.op("tensor", lambda e: e.transpose(
                psa_q[0][0:1, 0:32], qn1_sb[:, :], ident32_sb[0:32, 0:32]),
                reads=[trk("qn1"), T["c_ident32"]], writes=[trk("psa_q0")])
            def qred(e, hp=hp):
                e.reduce_max(qmax_sb[:, 2 * hp:2 * hp + 1],
                             psa_q[0][0:1, 0:32]
                             .rearrange("p (n s) -> p n s", s=2)[:, :, 0],
                             axis=AX.X)
                e.drain()
                r = e.reduce_max(qmax_sb[:, 2 * hp + 1:2 * hp + 2],
                                 psa_q[0][0:1, 0:32]
                                 .rearrange("p (n s) -> p n s", s=2)[:, :, 1],
                                 axis=AX.X)
                return r
            sA.op("vector", qred, reads=[trk("psa_q0")], writes=[trk("qmax")])
        backlog = []
        for fr, bk in hp_units:
            q_units.append(fr)
            if len(backlog) >= 2:
                q_units.append(backlog.pop(0))
            backlog.append(bk)
        q_units.extend(backlog)
        q_units.append(q_tail)
        hp_units = []

    qu_iter = iter(q_units)
    rate = len(q_units) / max(1, len(xf_fillers))
    acc = 0.0
    for f in xf_fillers:
        f()
        acc += rate
        while acc >= 1.0:
            qf = next(qu_iter, None)
            if qf is not None:
                qf()
            acc -= 1.0
    for qf in qu_iter:
        qf()

    with nc.Block("phA") as blkA:
        sA.emit(blkA, sems)
    esa.close()

    sB = Sched(C, SAT)
    esb = ExitStack()
    kT2 = [sbt(esb, f"kT2_{i}", [128, S], BF16) for i in range(2)]
    wkv_sb[1] = sbt(esb, "wkv_sb1", [128, 2 * NCC * 128], BF16)
    v_aug = [sbt(esb, f"v_aug{i}", [128, KC * 130], BF16) for i in range(2)]
    xf_b = [sbt(esb, f"xf_b{i}", [128, NCC * 512], BF16) for i in range(2)]
    wkv_f = [sbt(esb, f"wkv_f{i}", [128, NCC * 128], F32) for i in range(2)]
    kpre_sb = [sbt(esb, f"kpre{i}", [128, 512], BF16) for i in range(2)]
    rk_sb = [sbt(esb, f"rk{i}", [128, 512], BF16) for i in range(2)]
    bt1 = sbt(esb, "bt1", [128, 512], BF16)
    bt2 = sbt(esb, "bt2", [128, 512], BF16)
    vpre_sb = [sbt(esb, f"vpre{i}", [128, 512], BF16) for i in range(2)]
    ksq_sb = [sbt(esb, f"ksq{i}", [128, 512], BF16) for i in range(4)]
    knp_sb = sbt(esb, "knp_sb", [32, 4], F32)
    kn1_sb = sbt(esb, "kn1_sb", [32, 1], F32)
    pt_sb = sbt(esb, "pt_sb", [128, 4 * 1024], BF16)
    ctxT_f = sbt(esb, "ctxT_f", [128, 512], F32)
    bcf_sb = sbt(esb, "bcf_sb", [128, 2], F32)
    bnd_sb = sbt(esb, "bnd_sb", [2, 64], F32)
    rl_sb = sbt(esb, "rl_sb", [128, 4], F32)
    ctx_hp = sbt(esb, "ctx_hp", [128, NSUB * 128], F32)
    ps_st = [pst(esb, "ps_stA", [128, 1024]), pst(esb, "ps_stB", [128, 1024])]
    ps_c = [pst(esb, f"ps_c{i}", [128, 512]) for i in range(2)]
    ps_b6 = pst(esb, "ps_b6", [128, 512])
    ps_b7 = pst(esb, "ps_b7", [128, 512], BF16)

    sB.raw("sync", lambda e: e.nop(), waits=all_dma_waits(), inc=False)

    def emit_load_wkv(hp):
        buf = hp % 2
        for j, wd in enumerate((wk_d, wv_d)):
            dma(sB, wkv_f[buf][:, :].rearrange("p (c m) -> p c m", m=128),
                wd[:, hp * 128:(hp + 1) * 128]
                .rearrange("(c p) m -> p c m", p=128), "wkv",
                writes=[trk(f"wkv_f{buf}")])
            sB.op("vector", lambda e, buf=buf, j=j: e.tensor_copy(
                wkv_sb[buf][:, j * NCC * 128:(j + 1) * NCC * 128],
                wkv_f[buf][:, :]),
                reads=[trk(f"wkv_f{buf}")], writes=[trk(f"wkv{buf}")])

    def kv_kt_fillers(hp, kt, tagged=False):
        buf = hp % 2
        xb = kt % 2
        pp = kt % 2
        fl = []
        fl.append(lambda: dma(sB, xf_b[xb][:, :],
            xT_sp[:, kt * NCC * 512:(kt + 1) * NCC * 512], "xft",
            reads=[trk(f"xT_sp{kt}")], writes=[trk(f"xf_b{xb}")]))
        def kproj(e, buf=buf, xb=xb):
            r = None
            for cc in range(NCC):
                r = e.matmul(ps_b6[:, :],
                             wkv_sb[buf][:, cc * 128:(cc + 1) * 128],
                             xf_b[xb][:, cc * 512:(cc + 1) * 512],
                             start=(cc == 0), stop=(cc == NCC - 1))
            return r
        def _fd(emit):
            def g():
                if os.environ.get("KDBG_FDRAIN"):
                    sB._fdrain = True
                emit()
                sB._fdrain = False
            return g
        fl.append(_fd(lambda: sB.op("tensor", kproj,
            reads=[trk(f"wkv{buf}"), trk(f"xf_b{xb}")], writes=[trk("b6")])))
        fl.append(lambda: sB.op("vector", lambda e, pp=pp, hp=hp: e.tensor_scalar(
            kpre_sb[pp][:, :], ps_b6[:, :], bk_sb[:, hp:hp + 1], None, ALU.add),
            reads=[trk("b6"), T["c_bk"]], writes=[trk(f"kpre{pp}")]))
        fl.append(_fd(lambda: sB.op("tensor", lambda e, pp=pp: e.matmul(
            ps_b6[:, :], rt2_sb[:, :], kpre_sb[pp][:, :], start=True, stop=True),
            reads=[trk(f"kpre{pp}"), T["c_rt2"]], writes=[trk("b6")])))
        fl.append(lambda: sB.op("vector", lambda e, pp=pp: e.tensor_copy(
            rk_sb[pp][:, :], ps_b6[:, :]),
            reads=[trk("b6")], writes=[trk(f"rk{pp}")]))
        def ropek(e, pp=pp, buf=buf, kt=kt):
            e.tensor_tensor(bt1[:, :], rk_sb[pp][:, :],
                            sin2_sb[:, kt * 512:(kt + 1) * 512], ALU.mult)
            e.tensor_tensor(bt2[:, :], kpre_sb[pp][:, :],
                            cos2_sb[:, kt * 512:(kt + 1) * 512], ALU.mult)
            e.drain()
            r = e.tensor_tensor(kT2[buf][:, kt * 512:(kt + 1) * 512],
                                bt1[:, :], bt2[:, :], ALU.add)
            return r
        fl.append(lambda: sB.op("vector", ropek,
            reads=[trk(f"rk{pp}"), trk(f"kpre{pp}"), trk("cos2"), trk("sin2")],
            writes=[trk(f"kT2_{buf}"), trk("btx")]))
        def vproj(e, buf=buf, xb=xb):
            r = None
            for cc in range(NCC):
                r = e.matmul(ps_b6[:, :],
                             wkv_sb[buf][:, NCC * 128 + cc * 128:
                                         NCC * 128 + (cc + 1) * 128],
                             xf_b[xb][:, cc * 512:(cc + 1) * 512],
                             start=(cc == 0), stop=(cc == NCC - 1))
            return r
        fl.append(_fd(lambda: sB.op("tensor", vproj,
            reads=[trk(f"wkv{buf}"), trk(f"xf_b{xb}")], writes=[trk("b6")])))
        fl.append(lambda: sB.op("vector", lambda e, pp=pp, hp=hp: e.tensor_scalar(
            vpre_sb[pp][:, :], ps_b6[:, :], bv_sb[:, hp:hp + 1], None, ALU.add),
            reads=[trk("b6"), T["c_bv"]], writes=[trk(f"vpre{pp}")]))
        def vtr(e, pp=pp):
            r = None
            for s in range(4):
                r = e.matmul(ps_b7[:, s * 128:(s + 1) * 128],
                             vpre_sb[pp][:, s * 128:(s + 1) * 128],
                             ident_sb[:, :], start=True, stop=True,
                             is_transpose=True)
            return r
        fl.append(_fd(lambda: sB.op("tensor", vtr,
            reads=[trk(f"vpre{pp}"), T["c_ident"]], writes=[trk("b7")])))
        def vsplit(e, buf=buf, kt=kt):
            r = None
            for s in range(4):
                c = kt * 4 + s
                r = e.tensor_copy(
                    v_aug[buf][:, c * 130:(c + 1) * 130]
                    .rearrange("p (g n) -> p g n", g=2)[:, :, 0:64],
                    ps_b7[:, s * 128:(s + 1) * 128]
                    .rearrange("p (g n) -> p g n", g=2))
            return r
        fl.append(lambda: sB.op("vector", vsplit, reads=[trk("b7")],
                                writes=[trk(f"v_aug{buf}")]))
        if tagged:
            tags = ["dma", "kproj", "kevac", "rkmm", "rkevac", "rope",
                    "vproj", "vevac", "vtr", "vsplit"]
            assert len(tags) == len(fl)
            return list(zip(fl, tags))
        return fl

    def kv_finale_fillers(hp, tagged=False):
        buf = hp % 2
        fl = []
        fl.append(lambda: sB.op("vector", lambda e, buf=buf: e.memset(
            v_aug[buf].rearrange("p (c x) -> p c x", c=KC)[:, :, 64:65], 1.0),
            writes=[trk(f"v_aug{buf}")]))
        fl.append(lambda: sB.op("vector", lambda e, buf=buf: e.memset(
            v_aug[buf].rearrange("p (c x) -> p c x", c=KC)[:, :, 129:130], 1.0),
            writes=[trk(f"v_aug{buf}")]))

        NG = 4
        for g in range(0, KT, NG):
            for kt in range(g, min(g + NG, KT)):
                pp = kt % NG
                fl.append(lambda kt=kt, pp=pp: sB.op("gpsimd",
                    lambda e, buf=buf, kt=kt, pp=pp: e.tensor_tensor(
                        ksq_sb[pp][:, :], kT2[buf][:, kt * 512:(kt + 1) * 512],
                        kT2[buf][:, kt * 512:(kt + 1) * 512], ALU.mult),
                    reads=[trk(f"kT2_{buf}")], writes=[trk(f"ksq{pp}")]))
            def grp(g=g):
                n = min(NG, KT - g)
                def mmg(e, g=g, n=n):
                    r = None
                    for kt in range(g, g + n):
                        r = e.matmul(ps_b6[0:32, 0:512], sel_sb[:, kt, :],
                                     ksq_sb[kt % NG][:, :],
                                     start=(kt == g), stop=(kt == g + n - 1))
                    return r
                sB.op("tensor", mmg,
                      reads=[trk(f"ksq{i}") for i in range(NG)] + [T["c_sel"]],
                      writes=[trk("b6")])
                sB.op("vector", lambda e, g=g: e.reduce_max(
                    knp_sb[:, g // NG:g // NG + 1], ps_b6[0:32, 0:512],
                    axis=AX.X),
                    reads=[trk("b6")], writes=[trk("knp")])
            fl.append(grp)
        def tail():
            sB.op("vector", lambda e: e.reduce_max(
                kn1_sb[:, :], knp_sb[:, 0:(KT + 3) // 4], axis=AX.X),
                reads=[trk("knp")], writes=[trk("kn1")])
            sB.op("tensor", lambda e: e.transpose(
                ps_b6[0:1, 0:32], kn1_sb[:, :], ident32_sb[0:32, 0:32]),
                reads=[trk("kn1"), T["c_ident32"]], writes=[trk("b6")])
            def mred(e, hp=hp):
                e.reduce_max(mtmp_sb[:, 0:1],
                             ps_b6[0:1, 0:32].rearrange("p (n s) -> p n s", s=2)
                             [:, :, 0], axis=AX.X)
                e.drain()
                e.reduce_max(mtmp_sb[:, 1:2],
                             ps_b6[0:1, 0:32].rearrange("p (n s) -> p n s", s=2)
                             [:, :, 1], axis=AX.X)
                e.drain()
                r = e.tensor_tensor(mtmp_sb[:, 2:4], qmax_sb[:, 2 * hp:2 * hp + 2],
                                    mtmp_sb[:, 0:2], ALU.mult)
                return r
            sB.op("vector", mred, reads=[trk("b6"), trk("qmax")],
                  writes=[trk("mtmp")])
            def mact(e):
                e.activation(mtmp_sb[:, 4:6], mtmp_sb[:, 2:4], AF.Ln,
                             bias=0.0, scale=1.0)
                e.drain()
                r = e.activation(mtmp_sb[:, 6:8], mtmp_sb[:, 4:6], AF.Exp,
                                 bias=0.0, scale=0.5)
                return r
            sB.op("scalar", mact, reads=[trk("mtmp")], writes=[trk("mtmp")])
            def mneg(e):
                e.reduce_max(mtmp_sb[:, 0:1], mtmp_sb[:, 6:8], axis=AX.X)
                e.drain()
                e.tensor_scalar(mtmp_sb[:, 1:2], mtmp_sb[:, 0:1],
                                -1.02 / 8.0, None, ALU.mult)
                e.drain()
                r = e.tensor_copy(mneg_bf[:, 0:1], mtmp_sb[:, 1:2])
                return r
            sB.op("vector", mneg, reads=[trk("mtmp")],
                  writes=[trk("mtmp"), trk("mnegb")])
            sB.op("tensor", lambda e: e.matmul(
                ps_b6[:, 0:1], ones_sb[:, :], mneg_bf[:, 0:1], start=True,
                stop=True),
                reads=[trk("mnegb"), T["c_ones"]], writes=[trk("b6")])
            sB.op("vector", lambda e, hp=hp: e.tensor_copy(
                mhat_sb[:, hp:hp + 1], ps_b6[:, 0:1]),
                reads=[trk("b6")], writes=[trk(f"mhat{hp}")])
        fl.append(tail)
        if tagged:
            tags_full = ["ms", "ms"]
            for g in range(0, KT, 4):
                tags_full += ["sq"] * min(4, KT - g) + ["grp"]
            tags_full += ["tail"]
            assert len(tags_full) == len(fl), (len(tags_full), len(fl))
            return list(zip(fl, tags_full))
        return fl

    def emit_attention_qtile(hp, qi, fillers=iter(()), per_c=0):
        qoff, qn = cfg.QT[qi]
        buf = hp % 2
        acc = 0.0
        for c in range(KC):
            acc += per_c
            while acc >= 1.0:
                f = next(fillers, None)
                if f is not None:
                    f()
                acc -= 1.0
            g = c % 2
            def qkpv(e, hp=hp, qi=qi, qoff=qoff, qn=qn, c=c, g=g, buf=buf):
                r = None
                for h in range(2):
                    for half in range(2):
                        r = e.matmul(
                            ps_st[g].rearrange("p (b n) -> p b n", b=2)
                            [half * 64:half * 64 + 64, h, 0:qn],
                            kT2[buf][h * 64:h * 64 + 64,
                                     c * 128 + half * 64:c * 128 + half * 64 + 64],
                            qT2[h * 64:h * 64 + 64,
                                hp * FPAD + qoff:hp * FPAD + qoff + qn],
                            start=True, stop=True)
                if c >= 2:
                    cp = c - 2
                    for h in range(2):
                        r = e.matmul(ps_c[h][0:65, 0:qn],
                                     v_aug[buf].rearrange("p (k x) -> p k x",
                                                          k=KC)
                                     [:, cp, h * 65:h * 65 + 65],
                                     pt_sb[:, (cp % 4) * 1024 + h * qn:
                                           (cp % 4) * 1024 + (h + 1) * qn],
                                     start=(cp == 0), stop=(cp == KC - 1))
                return r
            rd = [trk(f"kT2_{buf}"), trk("qT2")]
            wr = [trk(f"st{g}")]
            if c >= 2:
                rd += [trk(f"pt{(c - 2) % 4}"), trk(f"v_aug{buf}")]
                wr += [trk("ctx0"), trk("ctx1")]
            sB.op("tensor", qkpv, reads=rd, writes=wr)
            if qn == 512:
                sB.op("scalar", lambda e, g=g, c=c, hp=hp: e.activation(
                    pt_sb[:, (c % 4) * 1024:(c % 4) * 1024 + 1024],
                    ps_st[g][:, 0:1024],
                    AF.Exp, bias=mhat_sb[:, hp:hp + 1], scale=0.125),
                    reads=[trk(f"st{g}"), trk(f"mhat{hp}")],
                    writes=[trk(f"pt{c % 4}")])
            else:
                sB.op("scalar", lambda e, g=g, c=c, qn=qn, hp=hp: e.activation(
                    pt_sb[:, (c % 4) * 1024:(c % 4) * 1024 + 2 * qn]
                    .rearrange("p (b n) -> p b n", b=2),
                    ps_st[g].rearrange("p (b n) -> p b n", b=2)[:, :, 0:qn],
                    AF.Exp, bias=mhat_sb[:, hp:hp + 1], scale=0.125),
                    reads=[trk(f"st{g}"), trk(f"mhat{hp}")],
                    writes=[trk(f"pt{c % 4}")])
        _pv(hp, qi, KC - 2)
        _pv(hp, qi, KC - 1)
        nsl = (qn + 127) // 128
        for h in range(2):
            sB.op("vector", lambda e, h=h, qn=qn: e.tensor_copy(
                ctxT_f[0:65, 0:qn], ps_c[h][0:65, 0:qn]),
                reads=[trk(f"ctx{h}")], writes=[trk("ctxTf")])
            def ctr(e, h=h, nsl=nsl, qn=qn):
                r = None
                for s2 in range(nsl):
                    n2 = min(128, qn - s2 * 128)
                    r = e.matmul(ps_c[h][0:n2, s2 * 65:s2 * 65 + 65],
                                 ctxT_f[0:65, s2 * 128:s2 * 128 + n2],
                                 ident32_sb[0:65, 0:65], start=True, stop=True,
                                 is_transpose=True)
                return r
            sB.op("tensor", ctr, reads=[trk("ctxTf"), T["c_ident32"]],
                  writes=[trk(f"ctx{h}")])
            def cdiv(e, h=h, qi=qi, qn=qn, nsl=nsl):
                e.reciprocal(rl_sb[:, 0:nsl],
                             ps_c[h][:, 0:nsl * 65]
                             .rearrange("p (s x) -> p s x", x=65)[:, :, 64])
                e.drain()
                r = None
                for s2 in range(nsl):
                    r = e.tensor_scalar(
                        ctx_hp[:, (qi * 4 + s2) * 128 + h * 64:
                               (qi * 4 + s2) * 128 + h * 64 + 64],
                        ps_c[h][:, s2 * 65:s2 * 65 + 64],
                        rl_sb[:, s2:s2 + 1], None, ALU.mult)
                return r
            sB.op("vector", cdiv, reads=[trk(f"ctx{h}")],
                  writes=[trk("ctx_hp"), trk("rl")])

    def _pv(hp, qi, c):
        qoff, qn = cfg.QT[qi]
        buf = hp % 2
        def pv(e, c=c, qn=qn, buf=buf):
            r = None
            for h in range(2):
                r = e.matmul(ps_c[h][0:65, 0:qn],
                             v_aug[buf].rearrange("p (k x) -> p k x", k=KC)
                             [:, c, h * 65:h * 65 + 65],
                             pt_sb[:, (c % 4) * 1024 + h * qn:
                                   (c % 4) * 1024 + (h + 1) * qn],
                             start=(c == 0), stop=(c == KC - 1))
            return r
        wr = [trk("ctx0"), trk("ctx1")] if c in (0, KC - 1) else []
        ev = sB.op("tensor", pv,
                   reads=[trk(f"pt{c % 4}"), trk(f"v_aug{buf}")],
                   writes=wr)
        if not wr:
            trk("ctx0").note_write(ev)
            trk("ctx1").note_write(ev)

    def emit_boundary(hp):
        buf = hp % 2
        qc = hp * FPAD + OWN
        def bqk(e, buf=buf, qc=qc):
            r = None
            for c in range(KC):
                for h in range(2):
                    r = e.matmul(ps_st[0][:, h * 512 + c:h * 512 + c + 1],
                                 kT2[buf][h * 64:(h + 1) * 64,
                                          c * 128:(c + 1) * 128],
                                 qT2[h * 64:(h + 1) * 64, qc:qc + 1],
                                 start=True, stop=True)
            return r
        sB.op("tensor", bqk, reads=[trk(f"kT2_{buf}"), trk("qT2")],
              writes=[trk("st0")])
        sB.op("scalar", lambda e, hp=hp: e.activation(
            pt_sb[:, 0:2 * KC].rearrange("p (b n) -> p b n", b=2),
            ps_st[0].rearrange("p (b n) -> p b n", b=2)[:, :, 0:KC],
            AF.Exp, bias=mhat_sb[:, hp:hp + 1], scale=0.125),
            reads=[trk("st0"), trk(f"mhat{hp}")], writes=[trk("pt0")])
        for h in range(2):
            def bpv(e, h=h, buf=buf):
                r = None
                for c in range(KC):
                    r = e.matmul(ps_c[h][0:65, 0:1],
                                 v_aug[buf].rearrange("p (k x) -> p k x", k=KC)
                                 [:, c, h * 65:h * 65 + 65],
                                 pt_sb[:, h * KC + c:h * KC + c + 1],
                                 start=(c == 0), stop=(c == KC - 1))
                return r
            sB.op("tensor", bpv, reads=[trk("pt0"), trk(f"v_aug{buf}")],
                  writes=[trk(f"ctx{h}")])
        def bcopy(e):
            e.tensor_copy(bcf_sb[0:65, 0:1], ps_c[0][0:65, 0:1])
            e.drain()
            r = e.tensor_copy(bcf_sb[0:65, 1:2], ps_c[1][0:65, 0:1])
            return r
        sB.op("vector", bcopy, reads=[trk("ctx0"), trk("ctx1")],
              writes=[trk("bcf")])
        sB.op("tensor", lambda e: e.transpose(
            ps_b6[0:2, 0:65], bcf_sb[0:65, 0:2], ident32_sb[0:65, 0:65]),
            reads=[trk("bcf"), T["c_ident32"]], writes=[trk("b6")])
        def bdiv(e):
            e.reciprocal(rl_sb[0:2, 0:1], ps_b6[0:2, 64:65])
            e.drain()
            r = e.tensor_scalar(bnd_sb[0:2, 0:64], ps_b6[0:2, 0:64],
                                rl_sb[0:2, 0:1], None, ALU.mult)
            return r
        sB.op("vector", bdiv, reads=[trk("b6")],
              writes=[trk("bnd"), trk("rl")])
        sub16 = OWN // 128
        sB.op("vector", lambda e, sub16=sub16: e.memset(
            ctx_hp[:, sub16 * 128:(sub16 + 1) * 128], 0.0),
            writes=[trk("ctx_hp")])
        for h in range(2):
            dma(sB, ctx_hp[0:1, sub16 * 128 + h * 64:sub16 * 128 + h * 64 + 64],
                bnd_sb[h:h + 1, 0:64], "bnd",
                reads=[trk("bnd")], writes=[trk("ctx_hp")], eng="gpsimd")

    def emit_ctx_spill(hp):
        dma(sB, ctx_sp[:, :, hp * 128:(hp + 1) * 128]
            .rearrange("s p d -> p s d"),
            ctx_hp[:, :].rearrange("p (s d) -> p s d", d=128), "csp",
            reads=[trk("ctx_hp")], writes=[trk("ctx_sp")])

    def kv0_prologue():
        NG = 4
        for f, tag in kv_finale_fillers(0, tagged=True):
            if tag == "ms":
                f()
        for kt in range(KT):
            xb = kt % 2
            pp = kt % 2
            dma(sB, xf_b[xb][:, :],
                xT_sp[:, kt * NCC * 512:(kt + 1) * NCC * 512], "xft",
                reads=[trk(f"xT_sp{kt}")], writes=[trk(f"xf_b{xb}")])
            def kproj(e, xb=xb):
                r = None
                for cc in range(NCC):
                    r = e.matmul(ps_b6[:, :],
                                 wkv_sb[0][:, cc * 128:(cc + 1) * 128],
                                 xf_b[xb][:, cc * 512:(cc + 1) * 512],
                                 start=(cc == 0), stop=(cc == NCC - 1))
                return r
            sB.op("tensor", kproj, reads=[trk("wkv0"), trk(f"xf_b{xb}")],
                  writes=[trk("b6")])
            def vproj(e, xb=xb):
                r = None
                for cc in range(NCC):
                    r = e.matmul(ps_c[0][:, :],
                                 wkv_sb[0][:, NCC * 128 + cc * 128:
                                           NCC * 128 + (cc + 1) * 128],
                                 xf_b[xb][:, cc * 512:(cc + 1) * 512],
                                 start=(cc == 0), stop=(cc == NCC - 1))
                return r
            sB.op("tensor", vproj, reads=[trk("wkv0"), trk(f"xf_b{xb}")],
                  writes=[trk("ctx0")])
            sB.op("vector", lambda e, pp=pp: e.tensor_scalar(
                kpre_sb[pp][:, :], ps_b6[:, :], bk_sb[:, 0:1], None, ALU.add),
                reads=[trk("b6"), T["c_bk"]], writes=[trk(f"kpre{pp}")])
            sB.op("vector", lambda e, pp=pp: e.tensor_scalar(
                vpre_sb[pp][:, :], ps_c[0][:, :], bv_sb[:, 0:1], None,
                ALU.add),
                reads=[trk("ctx0"), T["c_bv"]], writes=[trk(f"vpre{pp}")])
            sB.op("tensor", lambda e, pp=pp: e.matmul(
                ps_b6[:, :], rt2_sb[:, :], kpre_sb[pp][:, :], start=True,
                stop=True),
                reads=[trk(f"kpre{pp}"), T["c_rt2"]], writes=[trk("b6")])
            def vtr(e, pp=pp):
                r = None
                for s in range(4):
                    r = e.matmul(ps_b7[:, s * 128:(s + 1) * 128],
                                 vpre_sb[pp][:, s * 128:(s + 1) * 128],
                                 ident_sb[:, :], start=True, stop=True,
                                 is_transpose=True)
                return r
            sB.op("tensor", vtr, reads=[trk(f"vpre{pp}"), T["c_ident"]],
                  writes=[trk("b7")])
            sB.op("vector", lambda e, pp=pp: e.tensor_copy(
                rk_sb[pp][:, :], ps_b6[:, :]),
                reads=[trk("b6")], writes=[trk(f"rk{pp}")])
            def ropek(e, pp=pp, kt=kt):
                e.tensor_tensor(bt1[:, :], rk_sb[pp][:, :],
                                sin2_sb[:, kt * 512:(kt + 1) * 512], ALU.mult)
                e.tensor_tensor(bt2[:, :], kpre_sb[pp][:, :],
                                cos2_sb[:, kt * 512:(kt + 1) * 512], ALU.mult)
                e.drain()
                r = e.tensor_tensor(kT2[0][:, kt * 512:(kt + 1) * 512],
                                    bt1[:, :], bt2[:, :], ALU.add)
                return r
            sB.op("vector", ropek,
                  reads=[trk(f"rk{pp}"), trk(f"kpre{pp}"), trk("cos2"),
                         trk("sin2")],
                  writes=[trk("kT2_0"), trk("btx")])
            sB.op("gpsimd", lambda e, kt=kt: e.tensor_tensor(
                ksq_sb[kt % NG][:, :], kT2[0][:, kt * 512:(kt + 1) * 512],
                kT2[0][:, kt * 512:(kt + 1) * 512], ALU.mult),
                reads=[trk("kT2_0")], writes=[trk(f"ksq{kt % NG}")])
            def vsplit(e, kt=kt):
                r = None
                for s in range(4):
                    c = kt * 4 + s
                    r = e.tensor_copy(
                        v_aug[0][:, c * 130:(c + 1) * 130]
                        .rearrange("p (g n) -> p g n", g=2)[:, :, 0:64],
                        ps_b7[:, s * 128:(s + 1) * 128]
                        .rearrange("p (g n) -> p g n", g=2))
                return r
            sB.op("vector", vsplit, reads=[trk("b7")], writes=[trk("v_aug0")])
            if kt % NG == NG - 1 or kt == KT - 1:
                g = kt - kt % NG
                n = kt % NG + 1
                def mmg(e, g=g, n=n):
                    r = None
                    for k2 in range(g, g + n):
                        r = e.matmul(ps_st[0][0:32, 0:512], sel_sb[:, k2, :],
                                     ksq_sb[k2 % NG][:, :],
                                     start=(k2 == g), stop=(k2 == g + n - 1))
                    return r
                sB.op("tensor", mmg,
                      reads=[trk(f"ksq{i}") for i in range(NG)] + [T["c_sel"]],
                      writes=[trk("st0")])
                sB.op("vector", lambda e, g=g: e.reduce_max(
                    knp_sb[:, g // NG:g // NG + 1], ps_st[0][0:32, 0:512],
                    axis=AX.X),
                    reads=[trk("st0")], writes=[trk("knp")])
        for f, tag in kv_finale_fillers(0, tagged=True):
            if tag == "tail":
                f()

    kv0_prologue()
    for hp in range(HP):
        if hp + 1 < HP:
            emit_load_wkv(hp + 1)
            postk = []
            fillers = []
            postset = set(os.environ.get('KDBG_POSTSET', '').split(','))
            if os.environ.get('KDBG_POSTK'):
                postset = {"dma", "kproj", "kevac", "rkmm", "rkevac", "rope",
                           "vproj", "vevac", "vtr", "vsplit"}
            for kt in range(KT):
                for f, tag in kv_kt_fillers(hp + 1, kt, tagged=True):
                    (postk if tag in postset else fillers).append(f)
            postf = []
            for f, tag in kv_finale_fillers(hp + 1, tagged=True):
                if tag in ("grp", "tail"):
                    postf.append(f)
                elif os.environ.get('KDBG_POSTK'):
                    postk.append(f)
                else:
                    fillers.append(f)
        else:
            fillers = []
            postf = []
            postk = []
        nslots = KC * NQT
        per_c = (len(fillers) + 2) / nslots
        if os.environ.get('KDBG_NOILB'):
            per_c = 0
        nb = os.environ.get('KDBG_NBOUND')
        if nb is not None:
            nb = int(nb)
            postk = fillers[nb:] + postk
            fillers = fillers[:nb]
        fit = iter(fillers)
        nper_qt = (len(fillers) + NQT - 1) // NQT
        for qi in range(NQT):
            if os.environ.get('KDBG_QTBOUND'):
                emit_attention_qtile(hp, qi, iter(()), 0)
                for _ in range(nper_qt):
                    f = next(fit, None)
                    if f is not None:
                        f()
            else:
                emit_attention_qtile(hp, qi, fit, per_c)
        for f in fit:
            f()
        emit_boundary(hp)
        for f in postk:
            f()
        for f in postf:
            f()
        emit_ctx_spill(hp)

    with nc.Block("phB") as blkB:
        sB.emit(blkB, sems)
    esb.close()
    esq.close()

    sC = Sched(C, SAT)
    esc = ExitStack()
    x1T = sbt(esc, "x1T", [128, NCC * FRAME], F32)
    wr_sb = sbt(esc, "wr_sb", [128, NCC * D], BF16)
    wk2_sb = sbt(esc, "wk2_sb", [128, NCC * D], BF16)
    wv2_sb = sbt(esc, "wv2_sb", [128, NCC * D], BF16)
    xo_c = [sbt(esc, f"xo_c{i}", [128, D], F32) for i in range(2)]
    ctxg = [sbt(esc, f"ctxg{i}", [128, HP * 128], F32) for i in range(2)]
    st6 = [sbt(esc, f"st6_{i}", [128, 12], F32) for i in range(2)]
    mv2 = [sbt(esc, f"mv2_{i}", [128, 2], F32) for i in range(2)]
    lnv = [sbt(esc, f"lnv_{i}", [128, 1], F32) for i in range(2)]
    rstd = [sbt(esc, f"rstd_{i}", [128, 1], F32) for i in range(2)]
    nmr = [sbt(esc, f"nmr_{i}", [128, 1], F32) for i in range(2)]
    xr_sb = [sbt(esc, f"xr_sb{i}", [128, NCC * 512], BF16) for i in range(2)]
    xk_sb = [sbt(esc, f"xk_sb{i}", [128, NCC * 512], BF16) for i in range(2)]
    tmp_f = sbt(esc, "tmp_f", [128, 512], F32)
    tmp_g = sbt(esc, "tmp_g", [128, 512], F32)
    r_sb = sbt(esc, "r_sb", [128, NCC * 512], BF16)
    krelu = sbt(esc, "krelu", [128, NCC * 512], BF16)
    fs_sb = [sbt(esc, f"fs_sb{i}", [128, 512], F32) for i in range(2)]
    kact = krelu
    sumc = [sbt(esc, f"sumc{i}", [128, 512], F32) for i in range(2)]
    outn = sbt(esc, "outn", [128, 4 * D], F32)
    outf = xo_c
    ps_d = [pst(esc, f"ps_d{i}", [128, 512]) for i in range(2)]
    ps_t = [pst(esc, f"ps_t{i}", [128, 512]) for i in range(2)]

    sC.raw("sync", lambda e: e.nop(), waits=all_dma_waits(), inc=False)
    sC.raw("vector", lambda e: e.nop(), waits=all_dma_waits(), inc=False)
    def load_wffn_all():
        for j2, (wd, dst, tn) in enumerate(((wrt_d, wr_sb, "wr"),
                                            (wk2t_d, wk2_sb, "wk2"),
                                            (wv2t_d, wv2_sb, "wv2"))):
            for cc in range(NCC):
                b = (j2 * NCC + cc) % 4
                dma(sC, outn[:, b * D:(b + 1) * D],
                    wd[cc * 128:(cc + 1) * 128, :], "wf",
                    writes=[trk("outn")], parity=4)
                sC.op("gpsimd", lambda e, b=b, dst=dst, cc=cc: e.tensor_copy(
                    dst[:, cc * D:(cc + 1) * D], outn[:, b * D:(b + 1) * D]),
                    reads=[trk("outn")], writes=[trk(tn)])

    load_wffn_all()

    def ln_stats(s, src_ap, extra_reads, par):
        def stats(e, par=par):
            e.bn_stats(st6[par][:, 0:6], src_ap[:, 0:512])
            e.bn_stats(st6[par][:, 6:12], src_ap[:, 512:1024])
            e.drain()
            r = e.bn_aggr(mv2[par][:, :], st6[par][:, :])
            return r
        s.op("vector", stats, reads=extra_reads, writes=[trk(f"lnstat{par}")])
        def rstdf(e, par=par):
            e.activation(lnv[par][:, :], mv2[par][:, 1:2], AF.Ln,
                         bias=epsv_sb[:, 0:1], scale=1.0)
            e.drain()
            r = e.activation(rstd[par][:, :], lnv[par][:, :], AF.Exp, bias=0.0,
                             scale=-0.5)
            return r
        s.op("scalar", rstdf, reads=[trk(f"lnstat{par}"), T["c_epsv"]],
             writes=[trk(f"lnstat{par}")])
        s.op("vector", lambda e, par=par: e.tensor_scalar(
            nmr[par][:, :], mv2[par][:, 0:1], rstd[par][:, 0:1], -1.0,
            ALU.mult, ALU.mult),
            reads=[trk(f"lnstat{par}")], writes=[trk(f"lnstat{par}")])

    def emit_csub(s):
        b = s % 2
        dma(sC, xo_c[b][:, :], x_own_d[s * 128:(s + 1) * 128, :], "xoc",
            writes=[trk(f"xo_c{b}")])
        dma(sC, ctxg[b][:, :], ctx_sp[s, :, :], "cg",
            reads=[trk("ctx_sp")], writes=[trk(f"ctxg{b}")])
        sC.op("vector", lambda e, b=b: e.tensor_tensor(
            ctxg[b][:, :], xo_c[b][:, :], ctxg[b][:, :], ALU.add),
            reads=[trk(f"xo_c{b}")], writes=[trk(f"ctxg{b}")])
        ln_stats(sC, ctxg[b], [trk(f"ctxg{b}")], b)
        sC.op("vector", lambda e, b=b: e.tensor_scalar(
            ctxg[b][:, :], ctxg[b][:, :], rstd[b][:, 0:1], nmr[b][:, 0:1],
            ALU.mult, ALU.add),
            reads=[trk(f"lnstat{b}")], writes=[trk(f"ctxg{b}")])
        for half in range(2):
            pb = half
            def trc(e, b=b, half=half, pb=pb):
                r = None
                for i4 in range(4):
                    cc = half * 4 + i4
                    r = e.matmul(ps_t[pb][:, i4 * 128:(i4 + 1) * 128],
                                 ctxg[b][:, cc * 128:(cc + 1) * 128],
                                 ident32_sb[:, :], start=True, stop=True,
                                 is_transpose=True)
                return r
            sC.op("tensor", trc, reads=[trk(f"ctxg{b}"), T["c_ident32"]],
                  writes=[trk(f"ps_t{pb}")])
            tc = min(128, FRAME - s * 128)
            sC.op("scalar", lambda e, s=s, half=half, pb=pb, tc=tc: e.copy(
                x1T.rearrange("p (c t) -> p c t", c=NCC)
                [:, half * 4:half * 4 + 4, s * 128:s * 128 + tc],
                ps_t[pb][:, :].rearrange("p (c n) -> p c n", c=4)[:, :, 0:tc]),
                reads=[trk(f"ps_t{pb}")], writes=[trk(f"x1T_{s}")])
        if s == 0:
            sC.op("vector", lambda e: e.tensor_scalar(
                x1T.rearrange("p (c t) -> p c t", c=NCC)[:, :, 0:1],
                x1T.rearrange("p (c t) -> p c t", c=NCC)[:, :, 0:1],
                maskprev_sb[:, 0:1], None, ALU.mult),
                reads=[T["c_maskprev"]], writes=[trk("x1T_0")])

    def emit_mix(j):
        doff, dn = cfg.DT[j]
        jb = j % 2
        lo = (doff - 1) // 128
        hi = (doff + dn - 1) // 128
        xtags = [trk(f"x1T_{s}") for s in range(lo, hi + 1)]
        for cc in range(NCC):
            db = cc % 2
            dbuf = (tmp_f, tmp_g)[db]
            cur = x1T.rearrange("p (c t) -> p c t", c=NCC)[:, cc,
                                                          doff:doff + dn]
            prv = x1T.rearrange("p (c t) -> p c t", c=NCC)[:, cc,
                                                          doff - 1:
                                                          doff - 1 + dn]
            sC.op("gpsimd", lambda e, dbuf=dbuf, dn=dn, cur=cur, prv=prv:
                  e.tensor_tensor(dbuf[:, 0:dn], cur, prv, ALU.subtract),
                  reads=xtags, writes=[trk(f"dmix{db}")])

            def stt2(e, dbuf=dbuf, dn=dn, cc=cc, jb=jb, prv=prv):
                e.scalar_tensor_tensor(
                    xr_sb[jb][:, cc * 512:cc * 512 + dn], dbuf[:, 0:dn],
                    mur_sb[:, cc:cc + 1], prv, ALU.mult, ALU.add)
                r = e.scalar_tensor_tensor(
                    xk_sb[jb][:, cc * 512:cc * 512 + dn], dbuf[:, 0:dn],
                    muk_sb[:, cc:cc + 1], prv, ALU.mult, ALU.add)
                return r
            sC.op("vector", stt2,
                  reads=[trk(f"dmix{db}"), T["c_mur"], T["c_muk"]] + xtags,
                  writes=[trk(f"xr{jb}"), trk(f"xk{jb}")])

    _sub_state = [0]

    def ensure_subs(k):
        while _sub_state[0] <= k and _sub_state[0] < NSUB:
            emit_csub(_sub_state[0])
            _sub_state[0] += 1

    def _hi(j):
        return (cfg.DT[j][0] + cfg.DT[j][1] - 1) // 128

    ensure_subs(_hi(0))
    emit_mix(0)
    for j, (doff, dn) in enumerate(cfg.DT):
        jp = j % 2
        if j + 1 < len(cfg.DT):
            ensure_subs(_hi(j + 1))
            emit_mix(j + 1)
        else:
            ensure_subs(NSUB - 1)
        for kind in range(2):
            mov, mtn = ((xr_sb[jp], f"xr{jp}"), (xk_sb[jp], f"xk{jp}"))[kind]
            fn_act = (AF.Sigmoid, AF.Relu)[kind]
            dst, dtn = ((r_sb, "r_sb"), (krelu, "krelu"))[kind]
            wmat, wtn = ((wr_sb, "wr"), (wk2_sb, "wk2"))[kind]
            for oc in range(NCC):
                pb = (kind * NCC + oc) % 2
                def ff(e, wmat=wmat, mov=mov, oc=oc, pb=pb, dn=dn):
                    r = None
                    for cc in range(NCC):
                        r = e.matmul(ps_d[pb][:, 0:dn],
                                     wmat[:, cc * D + oc * 128:
                                          cc * D + oc * 128 + 128],
                                     mov[:, cc * 512:cc * 512 + dn],
                                     start=(cc == 0), stop=(cc == NCC - 1))
                    return r
                sC.op("tensor", ff, reads=[trk(wtn), trk(mtn)],
                      writes=[trk(f"ps_d{pb}")])
                sC.op("scalar", lambda e, dst=dst, oc=oc, pb=pb, dn=dn,
                      fn_act=fn_act: e.activation(
                          dst[:, oc * 512:oc * 512 + dn], ps_d[pb][:, 0:dn],
                          fn_act, bias=0.0, scale=1.0),
                      reads=[trk(f"ps_d{pb}")], writes=[trk(dtn)])
                if kind == 1:
                    sC.op("gpsimd", lambda e, oc=oc, dn=dn:
                          e.tensor_tensor(
                              kact[:, oc * 512:oc * 512 + dn],
                              krelu[:, oc * 512:oc * 512 + dn],
                              krelu[:, oc * 512:oc * 512 + dn], ALU.mult),
                          reads=[trk("krelu")],
                          writes=[trk("kact")])
        for oc in range(NCC):
            pb = oc % 2
            def ov(e, oc=oc, pb=pb, dn=dn):
                r = None
                for dc in range(NCC):
                    r = e.matmul(ps_d[pb][:, 0:dn],
                                 wv2_sb[:, dc * D + oc * 128:
                                        dc * D + oc * 128 + 128],
                                 kact[:, dc * 512:dc * 512 + dn],
                                 start=(dc == 0), stop=(dc == NCC - 1))
                return r
            sC.op("tensor", ov, reads=[trk("wv2"), trk("kact")],
                  writes=[trk(f"ps_d{pb}")])
            def comb(e, oc=oc, pb=pb, dn=dn, doff=doff):
                e.tensor_tensor(fs_sb[oc % 2][:, 0:dn], ps_d[pb][:, 0:dn],
                                r_sb[:, oc * 512:oc * 512 + dn], ALU.mult)
                e.drain()
                r = e.tensor_tensor(
                    sumc[oc % 2][:, 0:dn], fs_sb[oc % 2][:, 0:dn],
                    x1T.rearrange("p (c t) -> p c t", c=NCC)
                    [:, oc, doff:doff + dn], ALU.add)
                return r
            sC.op("vector", comb,
                  reads=[trk(f"ps_d{pb}"), trk("r_sb")]
                  + [trk(f"x1T_{s}") for s in range(doff // 128,
                                                    (doff + dn - 1) // 128 + 1)],
                  writes=[trk(f"sumc{oc % 2}"), trk(f"fs{oc % 2}")])
            nss = dn // 128
            pb2 = oc % 2
            def trd(e, oc=oc, nss=nss, pb2=pb2):
                r = None
                for s2 in range(nss):
                    r = e.matmul(ps_t[pb2][:, s2 * 128:(s2 + 1) * 128],
                                 sumc[oc % 2][:, s2 * 128:(s2 + 1) * 128],
                                 ident32_sb[:, :], start=True, stop=True,
                                 is_transpose=True)
                return r
            sC.op("tensor", trd, reads=[trk(f"sumc{oc % 2}"), T["c_ident32"]],
                  writes=[trk(f"ps_t{pb2}")])
            sC.op("vector", lambda e, oc=oc, nss=nss, pb2=pb2: e.tensor_copy(
                outn.rearrange("p (s d) -> p s d", s=4)
                [:, 0:nss, oc * 128:(oc + 1) * 128],
                ps_t[pb2][:, 0:nss * 128].rearrange("p (s n) -> p s n", n=128)),
                reads=[trk(f"ps_t{pb2}")], writes=[trk("outn")])
        for s2 in range(dn // 128):
            b = s2 % 2
            def stats2(e, s2=s2, b=b):
                e.bn_stats(st6[b][:, 0:6], outn[:, s2 * D:s2 * D + 512])
                e.bn_stats(st6[b][:, 6:12], outn[:, s2 * D + 512:(s2 + 1) * D])
                e.drain()
                r = e.bn_aggr(mv2[b][:, :], st6[b][:, :])
                return r
            sC.op("vector", stats2, reads=[trk("outn")],
                  writes=[trk(f"lnstat{b}")])
            def rstdf2(e, b=b):
                e.activation(lnv[b][:, :], mv2[b][:, 1:2], AF.Ln,
                             bias=epsv_sb[:, 0:1], scale=1.0)
                e.drain()
                r = e.activation(rstd[b][:, :], lnv[b][:, :], AF.Exp, bias=0.0,
                                 scale=-0.5)
                return r
            sC.op("scalar", rstdf2, reads=[trk(f"lnstat{b}"), T["c_epsv"]],
                  writes=[trk(f"lnstat{b}")])
            def ofin(e, b=b, s2=s2):
                e.tensor_scalar(nmr[b][:, :], mv2[b][:, 0:1], rstd[b][:, 0:1],
                                -1.0, ALU.mult, ALU.mult)
                e.drain()
                r = e.tensor_scalar(outf[b][:, :], outn[:, s2 * D:(s2 + 1) * D],
                                    rstd[b][:, 0:1], nmr[b][:, 0:1], ALU.mult,
                                    ALU.add)
                return r
            sC.op("vector", ofin, reads=[trk(f"lnstat{b}"), trk("outn")],
                  writes=[trk(f"xo_c{b}"), trk(f"lnstat{b}")])
            dma(sC, out_d[j * 512 + s2 * 128:j * 512 + (s2 + 1) * 128, :],
                outf[b][:, :], "out", reads=[trk(f"xo_c{b}")],
                writes=[trk("outd")])

    sC.raw("sync", lambda e: e.nop(), waits=all_dma_waits(), inc=False)

    with nc.Block("phCD") as blkC:
        sC.emit(blkC, sems)
    esc.close()
    es.close()
    return nc


def _host_constants(cfg):
    S, HD = cfg.S, 64
    pos = np.arange(S, dtype=np.float32)
    inv = (10000.0 ** (-2.0 * np.arange(HD // 2, dtype=np.float32) / HD))
    ang = pos[:, None] * inv[None, :].astype(np.float32)
    cos = np.repeat(np.cos(ang), 2, axis=-1).astype(np.float32)
    sin = np.repeat(np.sin(ang), 2, axis=-1).astype(np.float32)
    cos2 = np.tile(cos.T, (2, 1)).astype(BF)
    sin2 = np.tile(sin.T, (2, 1)).astype(BF)
    R = np.zeros((64, 64), np.float32)
    for i in range(32):
        R[2 * i, 2 * i + 1] = -1.0
        R[2 * i + 1, 2 * i] = 1.0
    RT2 = np.zeros((128, 128), np.float32)
    RT2[0:64, 0:64] = R.T
    RT2[64:128, 64:128] = R.T
    ident = np.eye(128, dtype=np.float32)
    sel = np.zeros((128, cfg.SELT, 32), np.float32)
    for t in range(cfg.SELT):
        sel[0:64, t, 2 * t] = 1.0
        sel[64:128, t, 2 * t + 1] = 1.0
    ones_row = np.ones((1, 128), np.float32)
    return (cos, sin, cos2, sin2, RT2.astype(BF), ident.astype(BF), ident,
            sel.astype(BF), ones_row.astype(BF))


def make_in_maps(cfg, x, Wq, bq, Wk, bk, Wv, bv, W_r, W_k2, W_v2, mu_r, mu_k,
                 ncores=NCORE):
    cos, sin, cos2, sin2, RT2, ident, ident32, sel, ones_row = \
        _host_constants(cfg)
    HP = cfg.HP
    com = {
        "wq": np.ascontiguousarray(Wq), "wk": np.ascontiguousarray(Wk),
        "wv": np.ascontiguousarray(Wv),
        "wrt": np.ascontiguousarray(W_r.T),
        "wk2t": np.ascontiguousarray(W_k2.T),
        "wv2t": np.ascontiguousarray(W_v2.T),
        "bq_t": np.ascontiguousarray(bq.reshape(HP, 128).T),
        "bk_t": np.ascontiguousarray(bk.reshape(HP, 128).T),
        "bv_t": np.ascontiguousarray(bv.reshape(HP, 128).T),
        "mur_t": np.ascontiguousarray(mu_r.reshape(cfg.NCC, 128).T),
        "muk_t": np.ascontiguousarray(mu_k.reshape(cfg.NCC, 128).T),
        "cos2": cos2, "sin2": sin2, "rt2": RT2, "ident": ident,
        "ident32": ident32, "sel": sel, "ones_row": ones_row,
        "epsv": np.full((128, 1), 1e-5, np.float32),
    }
    in_maps = []
    nquart = ncores // x.shape[0]
    for core in range(ncores):
        b, qtr = core // nquart, core % nquart
        t0 = qtr * cfg.OWN
        xo = np.zeros((cfg.FPAD, cfg.D), np.float32)
        xo[1:cfg.OWN + 1] = x[b, t0:t0 + cfg.OWN]
        if qtr > 0:
            xo[0] = x[b, t0 - 1]
        posf = np.clip(np.arange(cfg.FPAD) + (t0 - 1), 0, cfg.S - 1)
        cosq = np.tile(cos[posf].T, (2, 1)).astype(BF)
        sinq = np.tile(sin[posf].T, (2, 1)).astype(BF)
        m = dict(com)
        m["x_own"] = xo
        m["x_full"] = np.ascontiguousarray(x[b])
        m["cosq"] = cosq
        m["sinq"] = sinq
        m["maskprev"] = np.full((128, 1), 0.0 if qtr == 0 else 1.0, np.float32)
        in_maps.append(m)
    return in_maps


def kernel(L0_attn_hidden_states, L0_attn_Wq, L0_attn_bq, L0_attn_Wk,
           L0_attn_bk, L0_attn_Wv, L0_attn_bv, L0_ffn_W_r, L0_ffn_W_k,
           L0_ffn_W_v, L0_ffn_mu_r, L0_ffn_mu_k):
    cfg = Cfg()
    x = np.ascontiguousarray(np.asarray(L0_attn_hidden_states, np.float32))
    assert x.shape == (2, cfg.S, cfg.D)
    args = [np.asarray(a, np.float32) for a in
            (L0_attn_Wq, L0_attn_bq, L0_attn_Wk, L0_attn_bk, L0_attn_Wv,
             L0_attn_bv, L0_ffn_W_r, L0_ffn_W_k, L0_ffn_W_v, L0_ffn_mu_r,
             L0_ffn_mu_k)]
    Wq, bq, Wk, bk, Wv, bv, W_r, W_k2, W_v2, mu_r, mu_k = args
    in_maps = make_in_maps(cfg, x, Wq, bq, Wk, bk, Wv, bv, W_r, W_k2, W_v2,
                           mu_r, mu_k)
    nc = build_program(cfg)
    res = run_bass_kernel_spmd(nc, in_maps, core_ids=list(range(NCORE)))
    out = np.empty((2, cfg.S, cfg.D), np.float32)
    for core in range(NCORE):
        b, qtr = core // 4, core % 4
        out[b, qtr * cfg.OWN:(qtr + 1) * cfg.OWN] = res.results[core]["out"]
    return out
